# version 16 (final)
# baseline (speedup 1.0000x reference)
import numpy as np
import ml_dtypes
import concourse.bass as bass
import concourse.mybir as mybir
from concourse.bass_utils import run_bass_kernel_spmd
from concourse.bass_core_rust import IndirectOffsetOnAxis

F32 = mybir.dt.float32
BF16 = mybir.dt.bfloat16
I32 = mybir.dt.int32
U8 = mybir.dt.uint8
AF = mybir.ActivationFunctionType
ALU = mybir.AluOpType
AX = mybir.AxisListType

D = 1024
NCH = 8
TOK = 4096
W = 512
TC = 512
NCK = TOK // TC
HID = 4096
NHC = HID // 128
LH = 512
NLC = LH // 128
NSEL = 1024
RT = 64
NTO = (W + TOK) // 128
HUGE = 100000.0
EPS = 1e-5


class Buf:
    __slots__ = ("wd", "readers")

    def __init__(self):
        self.wd = {}
        self.readers = []


class Ctx:
    ENG = {"sp": "sync", "gp": "gpsimd", "pe": "tensor", "sc": "scalar",
           "ve": "vector", "s2": "scalar"}

    DMAQ = ("sp", "gp", "s2")
    NROT = 8

    def __init__(self, nc):
        self.nc = nc
        self.sem = {}
        self.cnt = {}
        self.lastwait = {}
        self.B = {}
        self.dix = {}
        for k in self.ENG:
            self.sem[k] = nc.semaphore("s_" + k).__enter__()
            self.cnt[k] = 0
        for q in self.DMAQ:
            self.dix[q] = 0
            for j in range(self.NROT):
                self.sem[(q, j)] = nc.semaphore("d_%s%d" % (q, j)).__enter__()
                self.cnt[(q, j)] = 0

    def eng(self, k):
        return getattr(self.nc, self.ENG[k])

    def buf(self, name):
        if name not in self.B:
            self.B[name] = Buf()
        return self.B[name]

    def _wait(self, ek, tick):
        dk, val, is_dma = tick
        if ek == dk == "pe" and not is_dma:
            return
        if self.lastwait.get((ek, dk), 0) >= val:
            return
        self.eng(ek).wait_ge(self.sem[dk], val)
        self.lastwait[(ek, dk)] = val

    def op(self, ek, emit, r=(), w=(), dma=False):
        reads = [self.buf(x) for x in r]
        writes = [self.buf(x) for x in w]
        for b in reads:
            for dk, val in b.wd.items():
                self._wait(ek, (dk, val, not isinstance(dk, str)))
        for b in writes:
            for dk, val in b.wd.items():
                self._wait(ek, (dk, val, not isinstance(dk, str)))
            for t in b.readers:
                self._wait(ek, t)
        ins = emit(self.eng(ek))
        if dma:
            sk = (ek, self.dix[ek] % self.NROT)
            self.dix[ek] += 1
            self.cnt[sk] += 16
            ins.then_inc(self.sem[sk], 16)
            t = (sk, self.cnt[sk], True)
        else:
            self.cnt[ek] += 1
            ins.then_inc(self.sem[ek], 1)
            t = (ek, self.cnt[ek], False)
        for b in reads:
            b.readers.append(t)
        for b in writes:
            b.wd[t[0]] = t[1]
            b.readers = []
        return t

    def final_wait(self, ek, dks=None):
        for dk, n in self.cnt.items():
            if n > 0:
                self.eng(ek).wait_ge(self.sem[dk], n)

    def fence(self):
        for q in self.DMAQ:
            for j in range(self.NROT):
                sk = (q, j)
                if self.cnt[sk] > self.lastwait.get((q, sk), 0):
                    self.eng(q).wait_ge(self.sem[sk], self.cnt[sk])
                    self.lastwait[(q, sk)] = self.cnt[sk]
        self.nc.all_engine_barrier()
        for ek in self.ENG:
            for dk in self.cnt:
                self.lastwait[(ek, dk)] = max(
                    self.lastwait.get((ek, dk), 0), self.cnt[dk])


BISECT_FOLDS = True


def build(dbg=False):
    nc = bass.Bass("TRN2", target_bir_lowering=False, debug=False)
    c = Ctx(nc)
    OP = c.op
    bcreg = nc.gpsimd.alloc_register("bcreg")
    nc.gpsimd.reg_mov(bcreg, NSEL - 1)

    def din(name, shape, dt=F32):
        return nc.dram_tensor(name, shape, dt, kind="ExternalInput").ap()

    xo = din("xo", [W + TOK, D])
    xp = din("xp", [RT * 128, D])
    wmask_i = din("wmask", [1, W])
    hs_i = din("hs", [1, 2])
    idf_i = din("idf", [128, 128])
    idb_i = din("idb", [128, 128], BF16)
    waT = din("waT", [D, D]); wbT = din("wbT", [D, D]); woT = din("woT", [D, D])
    cT = din("cT", [D, D]); pTw = din("pTw", [D, D])
    f1T = din("f1T", [D, HID]); f2T = din("f2T", [HID, D])
    lw1 = din("lw1", [D, LH]); lw2 = din("lw2", [LH, D])
    hw1 = din("hw1", [D, HID]); hw2 = din("hw2", [HID, D])
    bn = ["wab", "wbb", "wob", "cb", "lng", "lnb", "f2b", "pb", "lb2",
          "hb2", "hwe", "rtok", "lgam", "hgam"]
    b_in = {n: din(n, [D]) for n in bn}
    btall_i = din("btall", [128, 160])
    f1b_i = din("f1b", [HID]); hb1_i = din("hb1", [HID]); lb1_i = din("lb1", [LH])
    hsrow_i = din("hsrow", [1, 128])

    y = nc.dram_tensor("y", [TOK, D], F32, kind="ExternalOutput").ap()

    dk = "ExternalOutput" if dbg else "Internal"

    def dsc(name, shape, dt=BF16, kind="Internal"):
        return nc.dram_tensor(name, shape, dt, kind=kind).ap()

    wa16 = dsc("wa16", [D, D]); wb16 = dsc("wb16", [D, D]); wo16 = dsc("wo16", [D, D])
    c16 = dsc("c16", [D, D]); p16 = dsc("p16", [D, D])
    f116 = dsc("f116", [D, HID]); f216 = dsc("f216", [HID, D])
    lw116 = dsc("lw116", [D, LH]); lw216 = dsc("lw216", [LH, D])
    hw116 = dsc("hw116", [D, HID]); hw216 = dsc("hw216", [HID, D])
    xnrtm_d = dsc("xnrtm_d", [TOK, D], BF16, dk)
    xnrfm_d = dsc("xnrfm_d", [NCH, 128, TOK], BF16, dk)
    wpfm_d = dsc("wpfm_d", [NCH, 128, TOK], BF16, dk)
    comp_d = dsc("comp_d", [NSEL, D], BF16, dk)
    offs_flat_d = dsc("offs_flat_d", [TOK], mybir.dt.int32, "Internal")
    hvo_d = dsc("hvo_d", [NSEL, D], F32, dk)
    if dbg:
        sdbg_d = dsc("sdbg_d", [128, 5 * RT], F32, dk)
        hdbg_d = dsc("hdbg_d", [NCH, 128, TOK], BF16, dk)
        udbg_d = dsc("udbg_d", [NCH, 128, TOK], BF16, dk)
        ahist_d = dsc("ahist_d", [1, 64], F32, dk)
        zdbg_d = dsc("zdbg_d", [128, NCH * 512], F32, dk)
        wraw_d = dsc("wraw_d", [128, 512], F32, dk)
        btdbg_d = dsc("btdbg_d", [128, 16], F32, dk)
        wpsb_d = dsc("wpsb_d", [128, NCH * 512], BF16, dk)
        prowdbg_d = dsc("prowdbg_d", [1, 1536], F32, dk)
        st36_d = dsc("st36_d", [128, 8 * NTO], F32, dk)
        rwkdbg_d = dsc("rwkdbg_d", [128, 3 * RT], F32, dk)
        xndbg_d = dsc("xndbg_d", [W + TOK, D], BF16, dk)
        sclcbs_d = dsc("sclcbs_d", [128, 16], F32, dk)

    def alloc(name, shape, dt):
        return nc.alloc_sbuf_tensor(name, shape, dt).ap()

    rtn = ["s_pm", "sb_pm", "u_pm", "sat", "scores", "wk1", "wk2", "wk3",
           "wk4", "incl", "rank", "offsf", "selt", "su", "ones64"]
    offs_i = alloc("offs_i", [128, 32], I32)
    offstm_i = alloc("offstm_i", [128, 32], I32)
    scl = alloc("scl", [1, 64], F32)
    orow = alloc("orow", [1, 128], F32)
    onesq = alloc("onesq", [128, 128], F32)
    srow = alloc("srow", [1, 128], F32)
    cbs = alloc("cbs", [128, 8], F32)
    carry = alloc("carry", [128, NCH], F32)
    btall = alloc("btall_sb", [128, 160], F32)
    _btc = {"wab": (0, 8), "wbb": (8, 16), "wocb": (16, 24), "tmp8": (24, 32),
            "f2b": (32, 40), "pb": (40, 48), "hwe": (48, 56), "lgam": (56, 64),
            "hgam": (64, 72), "f1b": (72, 104), "hb1": (104, 136),
            "lb1": (136, 140), "wocb2": (140, 148)}
    BT = {n: btall[:, a:b] for n, (a, b) in _btc.items()}
    idf_t = alloc("idf_t", [128, 128], F32)
    idb_t = alloc("idb_t", [128, 128], BF16)
    hsb = alloc("hsb", [128, 2], F32)
    ones1 = alloc("ones1", [128, 1], F32)
    epscol = alloc("epscol", [128, 1], F32)
    lngrow = alloc("lngrow", [1, D], F32)
    lnbrow = alloc("lnbrow", [1, D], F32)
    lngrowb = alloc("lngrowb", [1, D], BF16)
    lnbrowb = alloc("lnbrowb", [1, D], BF16)
    orowb = alloc("orowb", [1, 128], BF16)
    ones1b = alloc("ones1b", [128, 1], BF16)

    AR_W = nc.alloc_sbuf_tensor("ar_w", [128, 49152], U8)
    AR_H = nc.alloc_sbuf_tensor("ar_h", [128, 65536], U8)
    AR_U = nc.alloc_sbuf_tensor("ar_u", [128, 65536], U8)
    AR_T = nc.alloc_sbuf_tensor("ar_t", [128, 16384], U8)

    def sub(arena, name, shape, dt, off):
        addr = nc.lookup_mloc(arena).addr + off
        return nc.alloc_sbuf_tensor_at(name, shape, dt, offset=addr).ap()

    wa_sb = sub(AR_W, "wa_sb", [128, NCH * 1024], BF16, 0)
    wb_sb = sub(AR_W, "wb_sb", [128, NCH * 1024], BF16, 16384)
    wo_sb = sub(AR_W, "wo_sb", [128, NCH * 1024], BF16, 32768)
    compfm = sub(AR_W, "compfm", [128, NCH * 1024], BF16, 0)
    hw1sl = sub(AR_W, "hw1sl", [128, NCH * 2048], BF16, 16384)
    hw2sl = sub(AR_W, "hw2sl", [128, 4 * 1024], BF16, 0)
    hvot = sub(AR_W, "hvot", [128, 2 * 1024], F32, 8192)
    hb2b = sub(AR_W, "hb2b", [1, D], BF16, 16384)
    cs_sb = sub(AR_W, "cs_sb", [128, NCH * 1024], BF16, 0)
    z_sb = sub(AR_W, "z_sb", [128, NCH * 512], F32, 16384)
    wp_sb = sub(AR_W, "wp_sb", [128, 2 * NCH * 512], BF16, 32768)
    pt_sb = sub(AR_W, "pt_sb", [128, NCH * 1024], BF16, 0)
    lw1_sb = sub(AR_W, "lw1_sb", [128, NCH * 512], BF16, 16384)
    lw2sl = sub(AR_W, "lw2sl", [128, 4 * 1024], BF16, 24576)
    xnrf_sb = sub(AR_W, "xnrf_sb", [128, NCH * 512], BF16, 32768)
    lh_sb = sub(AR_W, "lh_sb", [128, NLC * 512], BF16, 40960)
    xb2w = sub(AR_W, "xb2w", [128, 1024], F32, 45056)

    h_res = sub(AR_H, "h_res", [128, NCH * TOK], BF16, 0)
    f1h = sub(AR_H, "f1h", [128, NHC * 512], BF16, 0)
    f1sl = sub(AR_H, "f1sl", [128, NCH * 2048], BF16, 32768)
    f2sl = sub(AR_H, "f2sl", [128, 4 * 1024], BF16, 32768)

    big_u = sub(AR_U, "big_u", [128, NCH * TOK], BF16, 0)
    xbuf = sub(AR_U, "xbuf", [128, 2 * 1024], F32, 0)
    xbb0 = sub(AR_U, "xbb0", [128, 1024], BF16, 60672)
    xbb1 = sub(AR_U, "xbb1", [128, 1024], BF16, 62720)
    xntm = sub(AR_U, "xntm", [128, 1024], BF16, 8192)
    xnrtm = sub(AR_U, "xnrtm", [128, 1024], BF16, 10240)
    xnfm = sub(AR_U, "xnfm", [128, NCH * 512], BF16, 12288)
    xnrfm = sub(AR_U, "xnrfm", [128, NCH * 512], BF16, 20480)
    acc_s = sub(AR_U, "acc_s", [128, 512], F32, 28672)
    bcc_s = sub(AR_U, "bcc_s", [128, 512], F32, 30720)
    hcc_s = sub(AR_U, "hcc_s", [128, 512], F32, 32768)
    wpout = sub(AR_U, "wpout", [128, 2 * 512], BF16, 34816)
    sbuf2 = sub(AR_U, "sbuf2", [128, 2 * 1024], F32, 8192)
    gsl32 = sub(AR_U, "gsl32", [128, HID], F32, 16384)
    gsl16 = sub(AR_U, "gsl16", [128, HID], BF16, 32768)
    rbc = sub(AR_U, "rbc", [128, 1024], F32, 40960)
    rtrow = sub(AR_U, "rtrow", [1, 1024], F32, 45056)
    wmbc = sub(AR_U, "wmbc", [128, 512], F32, 49152)
    wmrow = sub(AR_U, "wmrow", [1, 512], F32, 51200)
    xr_s = sub(AR_U, "xr_s", [128, 1024], F32, 49152)
    scdump = sub(AR_U, "scdump", [128, 1024], F32, 60672)
    comptm = sub(AR_U, "comptm", [128, 2 * 1024], BF16, 0)
    RTT = {n: sub(AR_U, "rt_" + n, [128, RT], F32, 53248 + i * 256)
           for i, n in enumerate(rtn)}
    satu8 = sub(AR_U, "satu8", [128, RT], U8, 53248 + 15 * 256)
    statS = sub(AR_U, "statS", [128, NTO], F32, 57728)
    statQ = sub(AR_U, "statQ", [128, NTO], F32, 57888)
    tmp36a = sub(AR_U, "tmp36a", [128, NTO], F32, 58048)
    tmp36b = sub(AR_U, "tmp36b", [128, NTO], F32, 58208)
    muall = sub(AR_U, "muall", [128, NTO], F32, 58368)
    rstdall = sub(AR_U, "rstdall", [128, NTO], F32, 58528)
    rrall = sub(AR_U, "rrall", [128, NTO], F32, 58688)
    on2 = sub(AR_U, "on2", [2, 128], F32, 58880)
    ahist = sub(AR_U, "ahist", [1, 64], F32, 60416)
    bt2 = sub(AR_U, "bt2", [2, 128], F32, 59392)
    bs2 = sub(AR_U, "bs2", [2, 128], F32, 59904)

    uffn = sub(AR_T, "uffn", [128, NCH * 512], BF16, 0)
    ytile = sub(AR_T, "ytile", [128, 1024], F32, 8192)
    pbl2b = sub(AR_T, "pbl2b", [1, D], BF16, 12288)
    prow = sub(AR_T, "prow", [1, 4 * 512], F32, 0)
    prowb = sub(AR_T, "prowb", [1, 3 * 512], BF16, 8192)
    zb16 = sub(AR_T, "zb16", [128, 1024], BF16, 11264)
    zsqb = sub(AR_T, "zsqb", [128, 1024], BF16, 13312)

    ps_a = nc.alloc_psum_tensor("ps_a", [128, 512], F32).ap()
    ps_b = nc.alloc_psum_tensor("ps_b", [128, 512], F32).ap()
    ps_cd = nc.alloc_psum_tensor("ps_cd", [128, 1024], F32).ap()
    ps_tr = nc.alloc_psum_tensor("ps_tr", [128, 1024], BF16).ap()
    ps_s1 = nc.alloc_psum_tensor("ps_s1", [128, 512], F32).ap()
    ps_s2 = nc.alloc_psum_tensor("ps_s2", [1, 512], F32).ap()
    ps_x = nc.alloc_psum_tensor("ps_x", [128, 512], F32).ap()

    for nm, src, dst in [("wa16", waT, wa16), ("wb16", wbT, wb16),
                         ("wo16", woT, wo16), ("c16", cT, c16)]:
        OP("gp", lambda e, s=src, d=dst: e.dma_start(out=d, in_=s),
           w=[nm], dma=True)
    late_casts = [("f116", f1T, f116), ("f216", f2T, f216),
                  ("lw216", lw2, lw216), ("hw216", hw2, hw216),
                  ("p16", pTw, p16)]

    OP("sp", lambda e: e.dma_start(out=idf_t, in_=idf_i), w=["idf_t"], dma=True)
    OP("sp", lambda e: e.dma_start(out=idb_t, in_=idb_i), w=["idb_t"], dma=True)
    OP("ve", lambda e: e.tensor_scalar(orow, idf_t[0:1, :], 0.0, 1.0,
                                       ALU.mult, ALU.add),
       r=["idf_t"], w=["orow"])
    OP("ve", lambda e: e.tensor_scalar(onesq, idf_t, 0.0, 1.0,
                                       ALU.mult, ALU.add),
       r=["idf_t"], w=["onesq"])
    OP("ve", lambda e: e.tensor_scalar(ones1, idf_t[:, 0:1], 0.0, 1.0,
                                       ALU.mult, ALU.add),
       r=["idf_t"], w=["ones1"])
    OP("ve", lambda e: e.tensor_scalar(orowb, idf_t[0:1, :], 0.0, 1.0,
                                       ALU.mult, ALU.add),
       r=["idf_t"], w=["orowb"])
    OP("ve", lambda e: e.tensor_scalar(ones1b, idf_t[:, 0:1], 0.0, 1.0,
                                       ALU.mult, ALU.add),
       r=["idf_t"], w=["ones1b"])
    OP("ve", lambda e: e.tensor_scalar(epscol, idf_t[:, 0:1], 0.0, EPS,
                                       ALU.mult, ALU.add),
       r=["idf_t"], w=["epscol"])

    OP("sp", lambda e: e.dma_start(out=btall[:, 0:160], in_=btall_i),
       w=["btall"], dma=True)
    OP("ve", lambda e: e.tensor_tensor(BT["wocb2"], BT["wocb"], BT["tmp8"],
                                       ALU.add),
       r=["btall"], w=["btall"])

    OP("sp", lambda e: e.dma_start(out=lngrow, in_=b_in["lng"].rearrange("(o d) -> o d", o=1)),
       w=["lngrow"], dma=True)
    OP("sp", lambda e: e.dma_start(out=lnbrow, in_=b_in["lnb"].rearrange("(o d) -> o d", o=1)),
       w=["lnbrow"], dma=True)
    OP("sc", lambda e: e.copy(lngrowb, lngrow), r=["lngrow"], w=["lngrowb"])
    OP("sc", lambda e: e.copy(lnbrowb, lnbrow), r=["lnbrow"], w=["lnbrowb"])
    OP("sp", lambda e: e.dma_start(out=rtrow, in_=b_in["rtok"].rearrange("(o d) -> o d", o=1)),
       w=["rtrow"], dma=True)

    OP("sp", lambda e: e.dma_start(out=srow, in_=hsrow_i), w=["srow"], dma=True)
    OP("pe", lambda e: e.matmul(ps_x[:, 0:2], orow, srow[0:1, 0:2],
                                start=True, stop=True),
       r=["orow", "srow"], w=["ps_x"])
    OP("ve", lambda e: e.tensor_copy(hsb, ps_x[:, 0:2]), r=["ps_x"], w=["hsb"])


    for hf in range(2):
        OP("pe", lambda e, hf=hf: e.matmul(
            ps_cd[:, hf * 512:(hf + 1) * 512], orow,
            rtrow[0:1, hf * 512:(hf + 1) * 512], start=True, stop=True),
           r=["orow", "rtrow"], w=["ps_cd"])
    OP("ve", lambda e: e.tensor_copy(rbc[:, 0:512], ps_cd[:, 0:512]),
       r=["ps_cd"], w=["rbc"])
    OP("ve", lambda e: e.tensor_copy(rbc[:, 512:1024], ps_cd[:, 512:1024]),
       r=["ps_cd"], w=["rbc"])
    OP("ve", lambda e: e.reduce_sum(scl[0:1, 0:1], rtrow, axis=AX.X),
       r=["rtrow"], w=["scl"])
    OP("pe", lambda e: e.matmul(ps_x[:, 0:1], orow, scl[0:1, 0:1],
                                start=True, stop=True),
       r=["orow", "scl"], w=["ps_x"])
    OP("ve", lambda e: e.tensor_copy(cbs[:, 2:3], ps_x[:, 0:1]),
       r=["ps_x"], w=["cbs"])

    def fold_step(kc, heavy):
        if heavy:
            OP("sp", lambda e, kc=kc: e.dma_start(
                out=gsl32, in_=hw1[kc * 128:(kc + 1) * 128, :]),
                w=["gsl32"], dma=True)
            OP("ve", lambda e, kc=kc: e.tensor_scalar(
                gsl16, gsl32, BT["hgam"][:, kc:kc + 1], None, ALU.mult),
                r=["gsl32", "btall"], w=["gsl16"])
            OP("gp", lambda e, kc=kc: e.dma_start(
                out=hw116[kc * 128:(kc + 1) * 128, :], in_=gsl16),
                r=["gsl16"], w=["hw116"], dma=True)
        else:
            OP("sp", lambda e, kc=kc: e.dma_start(
                out=gsl32[:, 0:LH], in_=lw1[kc * 128:(kc + 1) * 128, :]),
                w=["gsl32"], dma=True)
            OP("ve", lambda e, kc=kc: e.tensor_scalar(
                gsl16[:, 0:LH], gsl32[:, 0:LH], BT["lgam"][:, kc:kc + 1], None,
                ALU.mult),
                r=["gsl32", "btall"], w=["gsl16"])
            OP("gp", lambda e, kc=kc: e.dma_start(
                out=lw116[kc * 128:(kc + 1) * 128, :], in_=gsl16[:, 0:LH]),
                r=["gsl16"], w=["lw116"], dma=True)
    fold_items = [(kc, True) for kc in range(NCH)] + \
                 [(kc, False) for kc in range(NCH)]

    def preduce(src_ap, src_buf, dst_slot, op=ALU.add, colslot=0):
        red = (lambda e: e.reduce_sum(cbs[:, colslot:colslot + 1], src_ap, axis=AX.X)) \
            if op == ALU.add else \
            (lambda e: e.tensor_reduce(cbs[:, colslot:colslot + 1], src_ap,
                                       axis=AX.X, op=op))
        OP("ve", red, r=[src_buf], w=["cbs"])
        OP("pe", lambda e: e.transpose(ps_s1[0:1, 0:128],
                                       cbs[:, colslot:colslot + 1], idf_t),
           r=["cbs", "idf_t"], w=["ps_s1"])
        OP("ve", lambda e: e.tensor_copy(srow, ps_s1[0:1, 0:128]),
           r=["ps_s1"], w=["srow"])
        red2 = (lambda e: e.reduce_sum(scl[0:1, dst_slot:dst_slot + 1], srow, axis=AX.X)) \
            if op == ALU.add else \
            (lambda e: e.tensor_reduce(scl[0:1, dst_slot:dst_slot + 1], srow,
                                       axis=AX.X, op=op))
        OP("ve", red2, r=["srow"], w=["scl"])

    def bcast(slot, col):
        OP("pe", lambda e: e.matmul(ps_s1[:, 0:1], orow,
                                    scl[0:1, slot:slot + 1], start=True, stop=True),
           r=["orow", "scl"], w=["ps_s1"])
        OP("ve", lambda e: e.tensor_copy(cbs[:, col:col + 1], ps_s1[:, 0:1]),
           r=["ps_s1"], w=["cbs"])

    def s_load(u):
        st = sbuf2[:, (u % 2) * 1024:(u % 2) * 1024 + 1024]
        OP("s2", lambda e, u=u, st=st: e.dma_start(
            out=st, in_=xo[u * 128:(u + 1) * 128, :]),
           w=["sbuf2%d" % (u % 2)], dma=True)

    s_load(0)
    s_load(1)
    for u in range(RT):
        xt = xbuf[:, (u % 2) * 1024:(u % 2) * 1024 + 1024]
        xb = "xbuf%d" % (u % 2)
        OP("sp", lambda e, u=u, xt=xt: e.dma_start(
            out=xt, in_=xp[u * 128:(u + 1) * 128, :]), w=[xb], dma=True)
        OP("sc", lambda e, xt=xt, u=u: e.activation(
            scdump, xt, AF.Copy, accum_out=RTT["wk1"][:, u:u + 1]),
           r=[xb], w=["dmyA%d" % u, "rt_wk1"])
        OP("sc", lambda e, xt=xt, u=u: e.activation(
            scdump, xt, AF.Square, accum_out=RTT["wk2"][:, u:u + 1]),
           r=[xb], w=["dmyB%d" % u, "rt_wk2"])
        OP("ve", lambda e, xt=xt: e.tensor_tensor(xr_s, xt, rbc, ALU.mult),
           r=[xb, "rbc"], w=["xr_s"])
        OP("ve", lambda e, u=u: e.reduce_sum(RTT["wk3"][:, u:u + 1], xr_s,
                                             axis=AX.X),
           r=["xr_s"], w=["rt_wk3"])
        if u < NTO:
            st = sbuf2[:, (u % 2) * 1024:(u % 2) * 1024 + 1024]
            sb3 = "sbuf2%d" % (u % 2)
            OP("sc", lambda e, st=st, u=u: e.activation(
                scdump, st, AF.Copy, accum_out=statS[:, u:u + 1]),
               r=[sb3], w=["dmyC%d" % u, "statS"])
            OP("sc", lambda e, st=st, u=u: e.activation(
                scdump, st, AF.Square, accum_out=statQ[:, u:u + 1]),
               r=[sb3], w=["dmyD%d" % u, "statQ"])
            if u + 2 < NTO:
                s_load(u + 2)
        if BISECT_FOLDS and u % 4 == 1 and u // 4 < len(fold_items):
            fold_step(*fold_items[u // 4])
        if u % 8 == 7 and (u // 8 - 3) >= 0 and (u // 8 - 3) < len(late_casts):
            nm4, s4, d4 = late_casts[u // 8 - 3]
            OP("gp", lambda e, s4=s4, d4=d4: e.dma_start(out=d4, in_=s4),
               w=[nm4], dma=True)
    if not BISECT_FOLDS:
        for fi in fold_items:
            fold_step(*fi)
    if dbg:
        for i5, nm5 in enumerate(["wk1", "wk2", "wk3"]):
            OP("gp", lambda e, i5=i5, nm5=nm5: e.dma_start(
                out=rwkdbg_d[:, i5 * RT:(i5 + 1) * RT], in_=RTT[nm5]),
               r=["rt_" + nm5], w=["rwkd"], dma=True)

    OP("ve", lambda e: e.tensor_scalar(RTT["wk1"], RTT["wk1"], 1.0 / D, None,
                                       ALU.mult), r=[], w=["rt_wk1"])
    OP("ve", lambda e: e.tensor_scalar(RTT["wk2"], RTT["wk2"], 1.0 / D, None,
                                       ALU.mult), r=[], w=["rt_wk2"])
    OP("ve", lambda e: e.tensor_tensor(RTT["wk4"], RTT["wk1"], RTT["wk1"],
                                       ALU.mult), r=["rt_wk1"], w=["rt_wk4"])
    OP("ve", lambda e: e.tensor_tensor(RTT["wk2"], RTT["wk2"], RTT["wk4"],
                                       ALU.subtract), r=["rt_wk4"], w=["rt_wk2"])
    OP("sc", lambda e: e.activation(RTT["wk2"], RTT["wk2"], AF.Ln, bias=epscol),
       r=["epscol"], w=["rt_wk2"])
    OP("sc", lambda e: e.activation(RTT["wk2"], RTT["wk2"], AF.Exp, scale=-0.5),
       r=[], w=["rt_wk2"])
    OP("ve", lambda e: e.tensor_scalar(RTT["wk4"], RTT["wk1"], cbs[:, 2:3],
                                       None, ALU.mult),
       r=["rt_wk1", "cbs"], w=["rt_wk4"])
    OP("ve", lambda e: e.tensor_tensor(RTT["wk3"], RTT["wk3"], RTT["wk4"],
                                       ALU.subtract), r=["rt_wk4"], w=["rt_wk3"])
    OP("ve", lambda e: e.tensor_tensor(RTT["s_pm"], RTT["wk3"], RTT["wk2"],
                                       ALU.mult), r=["rt_wk3", "rt_wk2"],
       w=["rt_s_pm"])

    OP("sc", lambda e: e.activation(RTT["scores"], RTT["s_pm"], AF.Exp),
       r=["rt_s_pm"], w=["rt_scores"])

    OP("ve", lambda e: e.tensor_scalar(gsl16[:, 0:1024], rbc, 0.0, None,
                                       ALU.mult), r=["rbc"], w=["gsl16"])
    for g in range(8):
        OP("gp", lambda e, g=g: e.dma_start(
            out=comp_d[g * 128:(g + 1) * 128, :], in_=gsl16[:, 0:1024]),
           r=["gsl16"], w=["comp_d"], dma=True)

    SELQ = []

    def OP(ek, emit, r=(), w=(), dma=False):
        SELQ.append((ek, emit, r, w, dma))

    OP("ve", lambda e: e.tensor_scalar(RTT["ones64"], RTT["s_pm"], 0.0, 1.0,
                                       ALU.mult, ALU.add),
       r=["rt_s_pm"], w=["rt_ones64"])
    OP("ve", lambda e: e.tensor_scalar(on2, idf_t[0:2, :], 0.0, 1.0,
                                       ALU.mult, ALU.add),
       r=["idf_t"], w=["on2"])

    OP("ve", lambda e: e.tensor_scalar(RTT["sb_pm"], RTT["s_pm"], 0.0, 1.0,
                                       ALU.mult, ALU.add),
       r=["rt_s_pm"], w=["rt_sb_pm"])
    LOGK = float(np.log(np.float32(1152.0), dtype=np.float32))
    INVK = float(np.float32(1.0) / np.float32(1152.0))
    for it in range(50):
        OP("ve", lambda e: e.reduce_sum(cbs[:, 0:1], RTT["sb_pm"], axis=AX.X),
           r=["rt_sb_pm"], w=["cbs"])
        OP("pe", lambda e: e.matmul(ps_s1[:, 1:2], onesq, cbs[:, 0:1],
                                    start=True, stop=True),
           r=["cbs", "onesq"], w=["ps_s1"])
        if it < 49:
            OP("ve", lambda e: e.tensor_scalar(cbs[:, 1:2], ps_s1[:, 1:2],
                                               INVK, None, ALU.mult),
               r=["ps_s1"], w=["cbs"])
            OP("ve", lambda e: e.tensor_scalar(RTT["sb_pm"], RTT["scores"],
                                               cbs[:, 1:2], None, ALU.min),
               r=["rt_scores", "cbs"], w=["rt_sb_pm"])
    OP("ve", lambda e: e.tensor_copy(scl[0:1, 0:1], ps_s1[0:1, 1:2]),
       r=["ps_s1"], w=["scl"])
    OP("sc", lambda e: e.activation(scl[0:1, 1:2], scl[0:1, 0:1], AF.Ln),
       r=["scl"], w=["scl"])
    OP("ve", lambda e: e.tensor_scalar(scl[0:1, 3:4], scl[0:1, 1:2], LOGK,
                                       -1.0, ALU.subtract, ALU.mult),
       r=[], w=["scl"])
    bcast(3, 1)
    OP("ve", lambda e: e.tensor_scalar(RTT["u_pm"], RTT["s_pm"], cbs[:, 1:2],
                                       None, ALU.add),
       r=["rt_s_pm", "cbs"], w=["rt_u_pm"])

    OP("ve", lambda e: e.tensor_scalar(RTT["sat"], RTT["u_pm"], 0.0, None,
                                       ALU.is_ge), r=["rt_u_pm"], w=["rt_sat"])
    OP("ve", lambda e: e.tensor_scalar(RTT["wk1"], RTT["u_pm"], 0.0, None,
                                       ALU.min), r=["rt_u_pm"], w=["rt_wk1"])
    OP("sc", lambda e: e.activation(RTT["scores"], RTT["wk1"], AF.Exp),
       r=["rt_wk1"], w=["rt_scores"])
    OP("ve", lambda e: e.reduce_sum(cbs[:, 0:1], RTT["sat"], axis=AX.X),
       r=["rt_sat"], w=["cbs"])
    OP("pe", lambda e: e.matmul(ps_s1[:, 1:2], onesq, cbs[:, 0:1],
                                start=True, stop=True),
       r=["cbs", "onesq"], w=["ps_s1"])
    OP("ve", lambda e: e.tensor_scalar(cbs[:, 3:4], ps_s1[:, 1:2], -1024.0,
                                       None, ALU.add), r=["ps_s1"], w=["cbs"])
    OP("ve", lambda e: e.tensor_scalar(cbs[:, 3:4], cbs[:, 3:4], 0.0,
                                       None, ALU.max), r=[], w=["cbs"])
    OP("ve", lambda e: e.tensor_tensor(cbs[:, 3:4], ps_s1[:, 1:2],
                                       cbs[:, 3:4], ALU.subtract),
       r=["ps_s1"], w=["cbs"])
    OP("ve", lambda e: e.tensor_scalar(cbs[:, 2:3], cbs[:, 3:4], -1.0, 1024.0,
                                       ALU.mult, ALU.add),
       r=[], w=["cbs"])

    def prefix2(src_ap, src_buf, dst_ap, dst_buf, glob):
        for hf in range(2):
            sl = slice(hf * 32, (hf + 1) * 32)
            OP("ve", lambda e, sl=sl: e.tensor_tensor_scan(
                RTT["incl"][:, sl], RTT["ones64"][:, sl], src_ap[:, sl], 0.0,
                ALU.mult, ALU.add), r=["rt_ones64", src_buf], w=["rt_incl"])
        OP("ve", lambda e: e.tensor_tensor(dst_ap, RTT["incl"], src_ap,
                                           ALU.subtract),
           r=["rt_incl", src_buf], w=[dst_buf])
        OP("ve", lambda e: e.tensor_copy(cbs[:, 5:6], RTT["incl"][:, 31:32]),
           r=["rt_incl"], w=["cbs"])
        OP("ve", lambda e: e.tensor_copy(cbs[:, 6:7], RTT["incl"][:, 63:64]),
           r=["rt_incl"], w=["cbs"])
        OP("pe", lambda e: e.transpose(ps_s1[0:2, 0:128], cbs[:, 5:7],
                                       idf_t),
           r=["cbs", "idf_t"], w=["ps_s1"])
        OP("ve", lambda e: e.tensor_copy(bt2, ps_s1[0:2, 0:128]),
           r=["ps_s1"], w=["bt2"])
        OP("ve", lambda e: e.tensor_tensor_scan(
            bs2, on2, bt2, 0.0, ALU.mult, ALU.add),
           r=["on2", "bt2"], w=["bs2"])
        OP("ve", lambda e: e.tensor_tensor(bs2, bs2, bt2, ALU.subtract),
           r=["bt2"], w=["bs2"])
        if glob:
            preduce(cbs[:, 5:6], "cbs", 8, colslot=7)
            bcast(8, 7)
        OP("pe", lambda e: e.transpose(ps_s1[:, 0:2],
                                       bs2, idf_t[0:2, 0:2]),
           r=["bs2", "idf_t"], w=["ps_s1"])
        OP("ve", lambda e: e.tensor_copy(cbs[:, 5:7], ps_s1[:, 0:2]),
           r=["ps_s1"], w=["cbs"])
        OP("ve", lambda e: e.tensor_scalar(dst_ap[:, 0:32], dst_ap[:, 0:32],
                                           cbs[:, 5:6], None, ALU.add),
           r=["cbs"], w=[dst_buf])
        if glob:
            OP("ve", lambda e: e.tensor_scalar(
                dst_ap[:, 32:64], dst_ap[:, 32:64], cbs[:, 6:7], cbs[:, 7:8],
                ALU.add, ALU.add), r=["cbs"], w=[dst_buf])
        else:
            OP("ve", lambda e: e.tensor_scalar(dst_ap[:, 32:64], dst_ap[:, 32:64],
                                               cbs[:, 6:7], None, ALU.add),
               r=["cbs"], w=[dst_buf])

    prefix2(RTT["sat"], "rt_sat", RTT["rank"], "rt_rank", glob=True)
    OP("ve", lambda e: e.tensor_scalar(RTT["wk1"], RTT["rank"], cbs[:, 3:4],
                                       None, ALU.is_lt),
       r=["rt_rank", "cbs"], w=["rt_wk1"])
    OP("ve", lambda e: e.tensor_tensor(RTT["selt"], RTT["sat"], RTT["wk1"],
                                       ALU.mult),
       r=["rt_sat", "rt_wk1"], w=["rt_selt"])

    OP("ve", lambda e: e.tensor_scalar(satu8, RTT["u_pm"], 0.0, None, ALU.is_ge),
       r=["rt_u_pm"], w=["satu8"])
    OP("ve", lambda e: e.tensor_scalar(RTT["wk2"], RTT["sat"], -1.0, None,
                                       ALU.mult), r=["rt_sat"], w=["rt_wk2"])
    OP("ve", lambda e: e.select(RTT["su"], satu8, RTT["wk2"], RTT["scores"]),
       r=["satu8", "rt_wk2", "rt_scores"], w=["rt_su"])
    wk4c = RTT["wk4"][:, 0:1]
    OP("ve", lambda e: e.tensor_scalar(cbs[:, 5:6], cbs[:, 2:3], 0.0,
                                       None, ALU.mult), r=[], w=["cbs"])
    OP("ve", lambda e: e.tensor_scalar(cbs[:, 6:7], cbs[:, 2:3], 0.0,
                                       1.5, ALU.mult, ALU.add), r=[], w=["cbs"])
    for it in range(35):
        OP("ve", lambda e: e.tensor_scalar(cbs[:, 4:5], cbs[:, 5:6],
                                           cbs[:, 6:7], 0.5, ALU.add, ALU.mult),
           r=[], w=["cbs"])
        OP("ve", lambda e: e.tensor_scalar(RTT["wk1"], RTT["su"], cbs[:, 4:5],
                                           None, ALU.is_ge),
           r=["rt_su", "cbs"], w=["rt_wk1"])
        OP("ve", lambda e: e.reduce_sum(cbs[:, 0:1], RTT["wk1"], axis=AX.X),
           r=["rt_wk1"], w=["cbs"])
        OP("pe", lambda e: e.matmul(ps_s1[:, 2:3], onesq, cbs[:, 0:1],
                                    start=True, stop=True),
           r=["cbs", "onesq"], w=["ps_s1"])
        OP("ve", lambda e: e.tensor_tensor(cbs[:, 7:8], ps_s1[:, 2:3],
                                           cbs[:, 2:3], ALU.is_ge),
           r=["ps_s1"], w=["cbs"])
        OP("ve", lambda e: e.tensor_tensor(wk4c, cbs[:, 4:5], cbs[:, 5:6],
                                           ALU.subtract), r=["cbs"], w=["rt_wk4"])
        OP("ve", lambda e: e.tensor_tensor(wk4c, wk4c, cbs[:, 7:8], ALU.mult),
           r=["cbs"], w=["rt_wk4"])
        OP("ve", lambda e: e.tensor_tensor(cbs[:, 5:6], cbs[:, 5:6], wk4c,
                                           ALU.add), r=["rt_wk4"], w=["cbs"])
        OP("ve", lambda e: e.tensor_scalar(cbs[:, 0:1], cbs[:, 7:8], -1.0, 1.0,
                                           ALU.mult, ALU.add), r=[], w=["cbs"])
        OP("ve", lambda e: e.tensor_tensor(wk4c, cbs[:, 6:7], cbs[:, 4:5],
                                           ALU.subtract), r=["cbs"], w=["rt_wk4"])
        OP("ve", lambda e: e.tensor_tensor(wk4c, wk4c, cbs[:, 0:1], ALU.mult),
           r=["cbs"], w=["rt_wk4"])
        OP("ve", lambda e: e.tensor_tensor(cbs[:, 6:7], cbs[:, 6:7], wk4c,
                                           ALU.subtract), r=["rt_wk4"], w=["cbs"])
    OP("ve", lambda e: e.tensor_scalar(RTT["wk1"], RTT["su"], cbs[:, 5:6],
                                       None, ALU.is_ge),
       r=["rt_su", "cbs"], w=["rt_wk1"])
    OP("ve", lambda e: e.tensor_scalar(RTT["wk2"], RTT["su"], cbs[:, 6:7],
                                       None, ALU.is_lt),
       r=["rt_su", "cbs"], w=["rt_wk2"])
    OP("ve", lambda e: e.tensor_tensor(RTT["wk1"], RTT["wk1"], RTT["wk2"],
                                       ALU.mult), r=["rt_wk2"], w=["rt_wk1"])
    OP("ve", lambda e: e.tensor_tensor(RTT["wk2"], RTT["su"], RTT["wk1"],
                                       ALU.mult), r=["rt_su", "rt_wk1"],
       w=["rt_wk2"])
    preduce(RTT["wk2"], "rt_wk2", 18, op=ALU.max)
    bcast(18, 4)
    OP("ve", lambda e: e.tensor_scalar(RTT["wk1"], RTT["su"], cbs[:, 4:5],
                                       None, ALU.is_gt),
       r=["rt_su", "cbs"], w=["rt_wk1"])
    OP("ve", lambda e: e.reduce_sum(cbs[:, 0:1], RTT["wk1"], axis=AX.X),
       r=["rt_wk1"], w=["cbs"])
    OP("pe", lambda e: e.matmul(ps_s1[:, 2:3], onesq, cbs[:, 0:1],
                                start=True, stop=True),
       r=["cbs", "onesq"], w=["ps_s1"])
    OP("ve", lambda e: e.tensor_copy(RTT["offsf"][:, 0:1], ps_s1[:, 2:3]),
       r=["ps_s1"], w=["rt_offsf"])
    OP("ve", lambda e: e.tensor_scalar(RTT["wk2"], RTT["su"], cbs[:, 4:5],
                                       None, ALU.is_equal),
       r=["rt_su", "cbs"], w=["rt_wk2"])
    prefix2(RTT["wk2"], "rt_wk2", RTT["rank"], "rt_rank", glob=True)
    OP("ve", lambda e: e.tensor_tensor(cbs[:, 3:4], cbs[:, 2:3],
                                       RTT["offsf"][:, 0:1], ALU.subtract),
       r=["rt_offsf"], w=["cbs"])
    OP("ve", lambda e: e.tensor_scalar(RTT["wk3"], RTT["rank"], cbs[:, 3:4],
                                       None, ALU.is_lt),
       r=["rt_rank", "cbs"], w=["rt_wk3"])
    OP("ve", lambda e: e.tensor_tensor(RTT["wk3"], RTT["wk3"], RTT["wk2"],
                                       ALU.mult), r=["rt_wk2"], w=["rt_wk3"])
    OP("ve", lambda e: e.tensor_tensor(RTT["wk3"], RTT["wk3"], RTT["wk1"],
                                       ALU.add), r=["rt_wk1"], w=["rt_wk3"])
    OP("ve", lambda e: e.tensor_scalar(cbs[:, 3:4], cbs[:, 2:3], 1.0,
                                       None, ALU.is_ge), r=[], w=["cbs"])
    OP("ve", lambda e: e.scalar_tensor_tensor(
        RTT["selt"], RTT["wk3"], cbs[:, 3:4], RTT["selt"], ALU.mult, ALU.add),
       r=["rt_wk3", "cbs"], w=["rt_selt"])

    prefix2(RTT["selt"], "rt_selt", RTT["rank"], "rt_rank", glob=False)
    OP("ve", lambda e: e.tensor_scalar(RTT["wk1"], RTT["selt"], -HUGE, HUGE,
                                       ALU.mult, ALU.add),
       r=["rt_selt"], w=["rt_wk1"])
    OP("ve", lambda e: e.tensor_tensor(RTT["offsf"], RTT["rank"], RTT["selt"],
                                       ALU.mult),
       r=["rt_rank", "rt_selt"], w=["rt_offsf"])
    OP("ve", lambda e: e.tensor_tensor(RTT["offsf"], RTT["offsf"], RTT["wk1"],
                                       ALU.add), r=["rt_wk1"], w=["rt_offsf"])
    OP("ve", lambda e: e.tensor_scalar(RTT["wk1"][:, 0:32], RTT["offsf"][:, 0:32],
                                       hsb[:, 0:1], None, ALU.mult),
       r=["rt_offsf", "hsb"], w=["rt_wk1"])
    OP("ve", lambda e: e.scalar_tensor_tensor(
        RTT["wk2"][:, 0:32], RTT["offsf"][:, 32:64], hsb[:, 1:2],
        RTT["wk1"][:, 0:32], ALU.mult, ALU.add),
       r=["rt_offsf", "hsb", "rt_wk1"], w=["rt_wk2"])
    OP("ve", lambda e: e.tensor_copy(offs_i, RTT["wk2"][:, 0:32]),
       r=["rt_wk2"], w=["offs_i"])
    OP("gp", lambda e: e.dma_start(
        out=offs_flat_d.rearrange("(p f) -> p f", f=32), in_=offs_i),
       r=["offs_i"], w=["offs_flat"], dma=True)
    with nc.allow_non_contiguous_dma(reason="one-time 16KB tile-major offs read"):
        OP("gp", lambda e: e.dma_start(
            out=offstm_i, in_=offs_flat_d.rearrange("(f p) -> p f", p=128),
            allow_slow_non_contiguous=True),
           r=["offs_flat"], w=["offstm_i"], dma=True)
    if dbg:
        OP("gp", lambda e: e.dma_start(out=btdbg_d[:, 0:8], in_=BT["wocb2"]),
           r=["btall"], w=["btd"], dma=True)
        OP("gp", lambda e: e.dma_start(out=btdbg_d[:, 8:16], in_=BT["hwe"]),
           r=["btall"], w=["btd"], dma=True)
        for i2, nm2 in enumerate(["s_pm", "u_pm", "selt", "offsf", "scores"]):
            OP("gp", lambda e, i2=i2, nm2=nm2: e.dma_start(
                out=sdbg_d[:, i2 * RT:(i2 + 1) * RT], in_=RTT[nm2]),
               r=["rt_" + nm2], w=["sdbg"], dma=True)
        OP("gp", lambda e: e.dma_start(out=ahist_d, in_=ahist),
           r=["ahist"], w=["ahd"], dma=True)
        OP("ve", lambda e: e.tensor_copy(cbs[:, 0:1], scl[0:1, 0:64].rearrange("o (a b) -> o a b", a=1) if False else cbs[:, 0:1]),
           r=[], w=["cbs"]) if False else None
        OP("gp", lambda e: e.dma_start(out=sclcbs_d[0:1, 0:16], in_=scl[0:1, 0:16]),
           r=["scl"], w=["scd"], dma=True)
        OP("gp", lambda e: e.dma_start(out=sclcbs_d[1:2, 0:16], in_=scl[0:1, 16:32]),
           r=["scl"], w=["scd"], dma=True)

    OP = c.op
    DRIP_K = len(SELQ) // 70 + 1

    def drip(n):
        while SELQ and n > 0:
            ek2, emit2, r2, w2, d2 = SELQ.pop(0)
            c.op(ek2, emit2, r=r2, w=w2, dma=d2)
            n -= 1

    BQ = list(range(4, NTO))

    def bounce_one(gt):
        xv, xb = (xbb0, "xbb0") if gt % 2 == 0 else (xbb1, "xbb1")
        c.op("sp", lambda e, gt=gt, xv=xv: e.dma_start(
            out=xv, in_=xnrtm_d[(gt - 4) * 128:(gt - 3) * 128, :]),
            r=["xnrtm_d"], w=[xb], dma=True)
        c.op("gp", lambda e, gt=gt, xv=xv: e.indirect_dma_start(
            out=comp_d,
            out_offset=IndirectOffsetOnAxis(offstm_i[:, gt - 4:gt - 3], 0),
            in_=xv, in_offset=None,
            bounds_check=bcreg, oob_is_err=False),
            r=[xb, "offstm_i", "comp_d"], w=["comp_d"], dma=True)

    def bdrip(max_gt, n=1):
        while BQ and not SELQ and BQ[0] <= max_gt and n > 0:
            bounce_one(BQ.pop(0))
            n -= 1

    c.fence()

    OP("ve", lambda e: e.tensor_scalar(muall, statS, 1.0 / D, None, ALU.mult),
       r=["statS"], w=["muall"])
    OP("ve", lambda e: e.tensor_scalar(tmp36a, statQ, 1.0 / D, None, ALU.mult),
       r=["statQ"], w=["tmp36a"])
    OP("ve", lambda e: e.tensor_tensor(tmp36b, muall, muall, ALU.mult),
       r=["muall"], w=["tmp36b"])
    OP("ve", lambda e: e.tensor_tensor(tmp36a, tmp36a, tmp36b, ALU.subtract),
       r=["tmp36b"], w=["tmp36a"])
    OP("sc", lambda e: e.activation(rstdall, tmp36a, AF.Ln, bias=epscol),
       r=["tmp36a", "epscol"], w=["rstdall"])
    OP("sc", lambda e: e.activation(rstdall, rstdall, AF.Exp, scale=-0.5),
       r=[], w=["rstdall"])
    OP("ve", lambda e: e.scalar_tensor_tensor(
        tmp36b, tmp36b, -float(D), statQ, ALU.mult, ALU.add),
       r=["statQ"], w=["tmp36b"])
    OP("ve", lambda e: e.tensor_tensor(tmp36a, rstdall, rstdall, ALU.mult),
       r=["rstdall"], w=["tmp36a"])
    OP("ve", lambda e: e.tensor_tensor(tmp36b, tmp36b, tmp36a, ALU.mult),
       r=["tmp36a"], w=["tmp36b"])
    OP("sc", lambda e: e.activation(tmp36b, tmp36b, AF.Ln), r=[], w=["tmp36b"])
    OP("sc", lambda e: e.activation(tmp36b, tmp36b, AF.Exp, scale=0.5),
       r=[], w=["tmp36b"])
    OP("ve", lambda e: e.tensor_scalar(tmp36b, tmp36b, 1e-12, None, ALU.max),
       r=[], w=["tmp36b"])
    OP("ve", lambda e: e.reciprocal(tmp36a, tmp36b), r=["tmp36b"], w=["tmp36a"])
    OP("sc", lambda e: e.copy(tmp36b, tmp36a), r=["tmp36a"], w=["tmp36b"])
    OP("ve", lambda e: e.scalar_tensor_tensor(
        rrall, tmp36b, 32.0, rstdall, ALU.mult, ALU.mult),
       r=["tmp36b", "rstdall"], w=["rrall"])
    OP("ve", lambda e: e.tensor_scalar(carry, BT["wab"], 0.0, None, ALU.mult),
       r=["btall"], w=["carry"])
    if dbg:
        for i3, (tt3, bb3) in enumerate([(statS, "statS"), (statQ, "statQ"),
                                         (tmp36a, "tmp36a"), (tmp36b, "tmp36b"),
                                         (muall, "muall"), (rstdall, "rstdall"),
                                         (rrall, "rrall")]):
            OP("gp", lambda e, i3=i3, tt3=tt3: e.dma_start(
                out=st36_d[:, i3 * NTO:(i3 + 1) * NTO], in_=tt3),
               r=[bb3], w=["st36"], dma=True)

    c.fence()

    for nm, src, dst in [("wa_sb", wa16, wa_sb), ("wb_sb", wb16, wb_sb),
                         ("wo_sb", wo16, wo_sb)]:
        OP("sp", lambda e, s=src, d=dst: e.dma_start(
            out=d.rearrange("p (kc m) -> p kc m", m=1024),
            in_=s.rearrange("(kc p) m -> p kc m", p=128)),
           r=[nm[:2] + "16"], w=[nm], dma=True)
    OP("sp", lambda e: e.dma_start(out=wmrow, in_=wmask_i), w=["wmrow"], dma=True)
    OP("pe", lambda e: e.matmul(ps_x, orow, wmrow, start=True, stop=True),
       r=["orow", "wmrow"], w=["ps_x"])
    OP("ve", lambda e: e.tensor_copy(wmbc, ps_x), r=["ps_x"], w=["wmbc"])

    for ck in range(NCK + 1):
        real = ck >= 1
        tokr = (ck - 1) * TC
        for i in range(4):
            gt = ck * 4 + i
            xt = xbuf[:, (i % 2) * 1024:(i % 2) * 1024 + 1024]
            xb = "xbuf%d" % (i % 2)
            OP("sp", lambda e, gt=gt, xt=xt: e.dma_start(
                out=xt, in_=xo[gt * 128:(gt + 1) * 128, :]), w=[xb], dma=True)
            drip(DRIP_K)
            OP("ve", lambda e, xt=xt, gt=gt: e.tensor_scalar(
                xntm, xt, muall[:, gt:gt + 1], rstdall[:, gt:gt + 1],
                ALU.subtract, ALU.mult),
               r=[xb, "muall", "rstdall"], w=["xntm"])
            if dbg:
                OP("gp", lambda e, gt=gt: e.dma_start(
                    out=xndbg_d[gt * 128:(gt + 1) * 128, :], in_=xntm),
                   r=["xntm"], w=["xnd"], dma=True)
            if real:
                OP("ve", lambda e, xt=xt, gt=gt: e.tensor_scalar(
                    xnrtm, xt, muall[:, gt:gt + 1], rrall[:, gt:gt + 1],
                    ALU.subtract, ALU.mult),
                   r=[xb, "muall", "rrall"], w=["xnrtm"])
                OP("gp", lambda e, gt=gt: e.dma_start(
                    out=xnrtm_d[(gt - 4) * 128:(gt - 3) * 128, :], in_=xnrtm),
                   r=["xnrtm"], w=["xnrtm_d"], dma=True)

            def trans8(e, src_ap=None):
                last = None
                for j in range(8):
                    last = e.transpose(ps_tr[:, j * 128:(j + 1) * 128],
                                       src_ap[:, j * 128:(j + 1) * 128], idb_t)
                return last

            def evac8(e, dst_ap=None, i=i):
                last = None
                for j in range(8):
                    last = e.copy(dst_ap[:, j * 512 + i * 128:j * 512 + i * 128 + 128],
                                  ps_tr[:, j * 128:(j + 1) * 128])
                return last

            OP("pe", lambda e: trans8(e, src_ap=xntm), r=["xntm", "idb_t"],
               w=["ps_tr"])
            OP("sc", lambda e: evac8(e, dst_ap=xnfm), r=["ps_tr"], w=["xnfm"])
            if real:
                OP("pe", lambda e: trans8(e, src_ap=xnrtm), r=["xnrtm", "idb_t"],
                   w=["ps_tr"])
                OP("sc", lambda e: evac8(e, dst_ap=xnrfm), r=["ps_tr"],
                   w=["xnrfm"])
        if real:
            OP("gp", lambda e, tokr=tokr: e.dma_start(
                out=xnrfm_d[:, :, tokr:tokr + TC].rearrange("cc p t -> p cc t"),
                in_=xnrfm.rearrange("p (cc t) -> p cc t", t=TC)),
               r=["xnrfm"], w=["xnrfm_d"], dma=True)

        def mm8(e, ps=None, slab=None, cc2=0, rhs=None):
            last = None
            for kc in range(NCH):
                last = e.matmul(ps,
                                slab[:, kc * 1024 + cc2 * 128:kc * 1024 + cc2 * 128 + 128],
                                rhs[:, kc * 512:(kc + 1) * 512],
                                start=(kc == 0), stop=(kc == NCH - 1))
            return last

        for cc in range(NCH):
            drip(DRIP_K)
            bdrip(ck * 4 + 3)
            OP("pe", lambda e, cc=cc: mm8(e, ps=ps_a, slab=wa_sb, cc2=cc,
                                          rhs=xnfm),
               r=["wa_sb", "xnfm"], w=["ps_a"])
            OP("sc", lambda e, cc=cc: e.activation(
                acc_s, ps_a, AF.Tanh, bias=BT["wab"][:, cc:cc + 1]),
               r=["ps_a", "btall"], w=["acc_s"])
            OP("pe", lambda e, cc=cc: mm8(e, ps=ps_b, slab=wb_sb, cc2=cc,
                                          rhs=xnfm),
               r=["wb_sb", "xnfm"], w=["ps_b"])
            OP("ve", lambda e, cc=cc: e.tensor_scalar(
                bcc_s, ps_b, BT["wbb"][:, cc:cc + 1], None, ALU.add),
               r=["ps_b", "btall"], w=["bcc_s"])
            if ck == 0:
                OP("ve", lambda e: e.tensor_tensor(acc_s, acc_s, wmbc, ALU.mult),
                   r=["wmbc"], w=["acc_s"])
                OP("ve", lambda e: e.tensor_tensor(bcc_s, bcc_s, wmbc, ALU.mult),
                   r=["wmbc"], w=["bcc_s"])

            OP("ve", lambda e, cc=cc: e.tensor_tensor_scan(
                hcc_s, acc_s, bcc_s, carry[:, cc:cc + 1], ALU.mult, ALU.add),
               r=["acc_s", "bcc_s", "carry"], w=["hcc_s"])
            OP("ve", lambda e, cc=cc: e.tensor_copy(carry[:, cc:cc + 1],
                                                    hcc_s[:, 511:512]),
               r=["hcc_s"], w=["carry"])
            if real:
                OP("sc", lambda e, cc=cc, tokr=tokr: e.copy(
                    h_res[:, cc * TOK + tokr:cc * TOK + tokr + TC], hcc_s),
                   r=["hcc_s"], w=["h_res"])
                wp_half = (cc % 2) * 512
                OP("pe", lambda e, cc=cc: mm8(e, ps=ps_x, slab=wo_sb, cc2=cc,
                                              rhs=xnfm),
                   r=["wo_sb", "xnfm"], w=["ps_x"])
                if dbg and ck == 1 and cc == 0:
                    OP("ve", lambda e: e.tensor_copy(xr_s[:, 0:512], ps_x),
                       r=["ps_x"], w=["xr_s"])
                    OP("gp", lambda e: e.dma_start(out=wraw_d, in_=xr_s[:, 0:512]),
                       r=["xr_s"], w=["wrawd"], dma=True)
                OP("ve", lambda e, cc=cc, wp_half=wp_half: e.tensor_scalar(
                    wpout[:, wp_half:wp_half + 512], ps_x,
                    BT["wocb2"][:, cc:cc + 1], BT["hwe"][:, cc:cc + 1],
                    ALU.add, ALU.mult),
                   r=["ps_x", "btall"], w=["wpout%d" % (cc % 2)])
                OP("gp", lambda e, cc=cc, tokr=tokr, wp_half=wp_half: e.dma_start(
                    out=wpfm_d[cc, :, tokr:tokr + TC],
                    in_=wpout[:, wp_half:wp_half + 512]),
                   r=["wpout%d" % (cc % 2)], w=["wpfm_d"], dma=True)
    drip(10 ** 9)
    bdrip(10 ** 9, n=10 ** 9)
    if dbg:
        OP("gp", lambda e: e.dma_start(
            out=hdbg_d.rearrange("cc p t -> p cc t"),
            in_=h_res.rearrange("p (cc t) -> p cc t", t=TOK)),
           r=["h_res"], w=["hdbg"], dma=True)

    c.fence()

    for g in range(8):
        ctm = comptm[:, (g % 2) * 1024:(g % 2) * 1024 + 1024]
        cb2 = "comptm%d" % (g % 2)
        OP("sp", lambda e, g=g, ctm=ctm: e.dma_start(
            out=ctm, in_=comp_d[g * 128:(g + 1) * 128, :]),
           r=["comp_d"], w=[cb2], dma=True)

        def tr8(e, ctm=ctm):
            last = None
            for j in range(8):
                last = e.transpose(ps_tr[:, j * 128:(j + 1) * 128],
                                   ctm[:, j * 128:(j + 1) * 128], idb_t)
            return last

        def ev8(e, g=g):
            last = None
            for j in range(8):
                last = e.copy(compfm[:, j * 1024 + g * 128:j * 1024 + g * 128 + 128],
                              ps_tr[:, j * 128:(j + 1) * 128])
            return last

        OP("pe", lambda e, ctm=ctm: tr8(e, ctm=ctm), r=[cb2, "idb_t"], w=["ps_tr"])
        OP("sc", ev8, r=["ps_tr"], w=["compfm"])

    c.fence()
    for hh2 in range(2):
        for kc in range(NCH):
            OP("sp", lambda e, kc=kc, hh2=hh2: e.dma_start(
                out=hw1sl[:, kc * 2048:(kc + 1) * 2048],
                in_=hw116[kc * 128:(kc + 1) * 128, hh2 * 2048:(hh2 + 1) * 2048]),
               r=["hw116"], w=["hw1sl%d" % kc], dma=True)
        PSH = (ps_a, ps_b, ps_cd[:, 0:512], ps_cd[:, 512:1024])
        PBH = ("ps_a", "ps_b", "ps_cd", "ps_cd")
        for og in range(4):
            for half in range(2):
                for q in range(4):
                    oq = og * 4 + q
                    occ = hh2 * 16 + oq
                    ps = PSH[q]
                    pbuf = PBH[q]

                    def mm1(e, oq=oq, half=half, ps=ps):
                        last = None
                        for kc in range(NCH):
                            last = e.matmul(
                                ps, hw1sl[:, kc * 2048 + oq * 128:kc * 2048 + oq * 128 + 128],
                                compfm[:, kc * 1024 + half * 512:kc * 1024 + half * 512 + 512],
                                start=(kc == 0), stop=(kc == NCH - 1))
                        return last

                    OP("pe", mm1,
                       r=["hw1sl%d" % kc3 for kc3 in range(NCH)] + ["compfm"],
                       w=[pbuf])
                    OP("sc", lambda e, occ=occ, ps=ps, half=half: e.activation(
                        big_u[:, occ * 1024 + half * 512:occ * 1024 + half * 512 + 512],
                        ps, AF.Gelu, bias=BT["hb1"][:, occ:occ + 1]),
                       r=[pbuf, "btall"], w=["big_u"])
    c.fence()
    OP("sp", lambda e: e.dma_start(
        out=hvot[0:1, 0:1024], in_=b_in["hb2"].rearrange("(o d) -> o d", o=1)),
       w=["hvot0"], dma=True)
    OP("sc", lambda e: e.copy(hb2b, hvot[0:1, 0:1024]), r=["hvot0"], w=["hb2b"])
    for g in range(8):
        for kc2 in range(NHC):
            sl = hw2sl[:, (kc2 % 4) * 1024:(kc2 % 4) * 1024 + 1024]
            sb2 = "hw2sl%d" % (kc2 % 4)
            OP("sp", lambda e, kc2=kc2, sl=sl: e.dma_start(
                out=sl, in_=hw216[kc2 * 128:(kc2 + 1) * 128, :]),
               r=["hw216"], w=[sb2], dma=True)

            def mmw2(e, kc2=kc2, g=g, sl=sl):
                last = None
                for half in range(2):
                    last = e.matmul(
                        ps_cd[:, half * 512:half * 512 + 512],
                        big_u[:, kc2 * 1024 + g * 128:kc2 * 1024 + g * 128 + 128],
                        sl[:, half * 512:half * 512 + 512],
                        start=(kc2 == 0), stop=False)
                return last

            OP("pe", mmw2, r=[sb2, "big_u"], w=["ps_cd"])
        def mmb2(e):
            e.matmul(ps_cd[:, 0:512], orowb, hb2b[0:1, 0:512],
                     start=False, stop=True, skip_group_check=True)
            return e.matmul(ps_cd[:, 512:1024], orowb, hb2b[0:1, 512:1024],
                            start=False, stop=True, skip_group_check=True)

        OP("pe", mmb2, r=["orowb", "hb2b"], w=["ps_cd"])
        hv = hvot[:, (g % 2) * 1024:(g % 2) * 1024 + 1024]
        hb3 = "hvot%d" % (g % 2)
        OP("sc", lambda e, hv=hv: e.copy(hv, ps_cd), r=["ps_cd"], w=[hb3])
        OP("gp", lambda e, g=g, hv=hv: e.dma_start(
            out=hvo_d[g * 128:(g + 1) * 128, :], in_=hv),
           r=[hb3], w=["hvo_d"], dma=True)

    c.fence()

    OP("sp", lambda e: e.dma_start(
        out=cs_sb.rearrange("p (kc m) -> p kc m", m=1024),
        in_=c16.rearrange("(kc p) m -> p kc m", p=128)),
       r=["c16"], w=["cs_sb"], dma=True)
    OP("ve", lambda e: e.tensor_scalar(prow[0:1, 1024:1536], lngrow[0:1, 0:512],
                                       0.0, 1.0, ALU.mult, ALU.add),
       r=["lngrow"], w=["prow"])
    OP("sc", lambda e: e.copy(prowb[0:1, 1024:1536], prow[0:1, 1024:1536]),
       r=["prow"], w=["prowb"])
    for ck in range(NCK):
        tokr = ck * TC
        wps = wp_sb[:, (ck % 2) * 4096:(ck % 2) * 4096 + 4096]
        wpb = "wp_sb%d" % (ck % 2)
        OP("sp", lambda e, tokr=tokr, wps=wps: e.dma_start(
            out=wps.rearrange("p (cc t) -> p cc t", t=512),
            in_=wpfm_d[:, :, tokr:tokr + TC].rearrange("cc p t -> p cc t")),
           r=["wpfm_d"], w=[wpb], dma=True)

        def mmc(e, cc=0, tokr=tokr, ps=None):
            last = None
            for kc in range(NCH):
                last = e.matmul(
                    ps, cs_sb[:, kc * 1024 + cc * 128:kc * 1024 + cc * 128 + 128],
                    h_res[:, kc * TOK + tokr:kc * TOK + tokr + TC],
                    start=(kc == 0), stop=(kc == NCH - 1))
            return last

        def mmst(e, cc=0):
            zp = (cc % 2) * 512
            e.matmul(ps_s1[0:1, :], ones1b, zb16[:, zp:zp + 512],
                     start=(cc == 0), stop=(cc == NCH - 1),
                     skip_group_check=True)
            return e.matmul(ps_s2, ones1b, zsqb[:, zp:zp + 512],
                            start=(cc == 0), stop=(cc == NCH - 1),
                            skip_group_check=True)

        for cc in range(NCH + 1):
            if cc < NCH:
                ps = (ps_a, ps_b)[cc % 2]
                pbuf = ("ps_a", "ps_b")[cc % 2]
                OP("pe", lambda e, cc=cc, ps=ps: mmc(e, cc=cc, ps=ps),
                   r=["cs_sb", "h_res"], w=[pbuf])
                OP("ve", lambda e, cc=cc, ps=ps, wps=wps: e.scalar_tensor_tensor(
                    z_sb[:, cc * 512:(cc + 1) * 512], ps, BT["hwe"][:, cc:cc + 1],
                    wps[:, cc * 512:(cc + 1) * 512], ALU.mult, ALU.add),
                   r=[pbuf, "btall", wpb], w=["z_sb"])
                zp = (cc % 2) * 512
                OP("sc", lambda e, cc=cc, zp=zp: e.copy(
                    zb16[:, zp:zp + 512], z_sb[:, cc * 512:(cc + 1) * 512]),
                   r=["z_sb"], w=["zb16"])
                OP("sc", lambda e, cc=cc, zp=zp: e.activation(
                    zsqb[:, zp:zp + 512], z_sb[:, cc * 512:(cc + 1) * 512],
                    AF.Square),
                   r=["z_sb"], w=["zsqb"])
            if cc >= 1:
                OP("pe", lambda e, cc=cc: mmst(e, cc=cc - 1),
                   r=["ones1b", "zb16", "zsqb"], w=["ps_s1", "ps_s2"])
        OP("ve", lambda e: e.tensor_scalar(prow[0:1, 1536:2048], ps_s1[0:1, :], 1.0 / D,
                                           None, ALU.mult),
           r=["ps_s1"], w=["prow"])
        OP("ve", lambda e: e.tensor_scalar(prow[0:1, 512:1024], ps_s2, 1.0 / D,
                                           None, ALU.mult),
           r=["ps_s2"], w=["prow"])
        OP("ve", lambda e: e.tensor_tensor(prow[0:1, 0:512], prow[0:1, 1536:2048],
                                           prow[0:1, 1536:2048], ALU.mult),
           r=[], w=["prow"])
        OP("ve", lambda e: e.tensor_tensor(prow[0:1, 512:1024], prow[0:1, 512:1024],
                                           prow[0:1, 0:512], ALU.subtract),
           r=[], w=["prow"])
        OP("sc", lambda e: e.activation(prow[0:1, 0:512], prow[0:1, 512:1024],
                                        AF.Ln, bias=epscol[0:1, 0:1]),
           r=["epscol"], w=["prow"])
        OP("sc", lambda e: e.activation(prow[0:1, 0:512], prow[0:1, 0:512],
                                        AF.Exp, scale=-0.5), r=[], w=["prow"])
        OP("ve", lambda e: e.tensor_tensor(prow[0:1, 512:1024], prow[0:1, 1536:2048],
                                           prow[0:1, 0:512], ALU.mult),
           r=[], w=["prow"])
        OP("ve", lambda e: e.tensor_scalar(prow[0:1, 512:1024], prow[0:1, 512:1024],
                                           -1.0, None, ALU.mult), r=[], w=["prow"])
        OP("sc", lambda e: e.copy(prowb[0:1, 0:512], prow[0:1, 0:512]),
           r=["prow"], w=["prowb"])
        OP("sc", lambda e: e.copy(prowb[0:1, 512:1024], prow[0:1, 512:1024]),
           r=["prow"], w=["prowb"])
        for cc in range(NCH):
            def mmpq(e, cc=cc):
                e.matmul(ps_cd[:, 0:512], lngrowb[0:1, cc * 128:(cc + 1) * 128],
                         prowb[0:1, 0:512], start=True, stop=False)
                e.matmul(ps_cd[:, 0:512], orowb, prowb[0:1, 1024:1536],
                         start=False, stop=True, skip_group_check=True)
                e.matmul(ps_cd[:, 512:1024], lngrowb[0:1, cc * 128:(cc + 1) * 128],
                         prowb[0:1, 512:1024], start=True, stop=False)
                return e.matmul(ps_cd[:, 512:1024],
                                lnbrowb[0:1, cc * 128:(cc + 1) * 128],
                                prowb[0:1, 1024:1536], start=False, stop=True,
                                skip_group_check=True)

            OP("pe", mmpq, r=["lngrowb", "lnbrowb", "orowb", "prowb"],
               w=["ps_cd"])
            OP("ve", lambda e, cc=cc: e.tensor_tensor(
                zsqb.bitcast(F32)[:, 0:512] if False else z_sb[:, cc * 512:(cc + 1) * 512],
                z_sb[:, cc * 512:(cc + 1) * 512], ps_cd[:, 0:512], ALU.mult),
               r=["ps_cd"], w=["z_sb"])
            OP("ve", lambda e, cc=cc, tokr=tokr: e.tensor_tensor(
                big_u[:, cc * TOK + tokr:cc * TOK + tokr + TC],
                z_sb[:, cc * 512:(cc + 1) * 512], ps_cd[:, 512:1024],
                ALU.add), r=["z_sb", "ps_cd"], w=["big_u"])
    if dbg:
        OP("gp", lambda e: e.dma_start(
            out=udbg_d.rearrange("cc p t -> p cc t"),
            in_=big_u.rearrange("p (cc t) -> p cc t", t=TOK)),
           r=["big_u"], w=["udbg"], dma=True)

    c.fence()

    OP("sp", lambda e: e.dma_start(
        out=xb2w[0:1, 0:1024], in_=b_in["pb"].rearrange("(o d) -> o d", o=1)),
       w=["xb2ws"], dma=True)
    OP("sp", lambda e: e.dma_start(
        out=ytile[0:1, 0:1024], in_=b_in["lb2"].rearrange("(o d) -> o d", o=1)),
       w=["ytile"], dma=True)
    OP("ve", lambda e: e.tensor_tensor(ytile[0:1, 0:1024], xb2w[0:1, 0:1024],
                                       ytile[0:1, 0:1024], ALU.add),
       r=["xb2ws"], w=["ytile"])
    OP("sc", lambda e: e.copy(pbl2b, ytile[0:1, 0:1024]),
       r=["ytile"], w=["pbl2b"])
    OP("sp", lambda e: e.dma_start(
        out=lw1_sb.rearrange("p (kc m) -> p kc m", m=512),
        in_=lw116.rearrange("(kc p) m -> p kc m", p=128)),
       r=["lw116"], w=["lw1_sb"], dma=True)
    def f1load(kc, hh2, q="sp"):
        wl1 = (["f1sl0", "f2sl0", "f2sl1"] if kc == 0 else
               ["f1sl1", "f2sl2", "f2sl3"] if kc == 1 else
               ["f1sl%d" % kc])
        OP(q, lambda e, kc=kc, hh2=hh2: e.dma_start(
            out=f1sl[:, kc * 2048:(kc + 1) * 2048],
            in_=f116[kc * 128:(kc + 1) * 128, hh2 * 2048:(hh2 + 1) * 2048]),
           r=["f116"], w=wl1, dma=True)

    for kc in range(NCH):
        f1load(kc, 0)
    for ck in range(NCK):
        tokr = ck * TC
        OP("sp", lambda e, tokr=tokr: e.dma_start(
            out=xnrf_sb.rearrange("p (cc t) -> p cc t", t=512),
            in_=xnrfm_d[:, :, tokr:tokr + TC].rearrange("cc p t -> p cc t")),
           r=["xnrfm_d"], w=["xnrf_sb"], dma=True)
        def light1(lc):
            ps = (ps_a, ps_b)[lc % 2]
            pbuf = ("ps_a", "ps_b")[lc % 2]

            def mml1(e, lc=lc, ps=ps):
                last = None
                for kc in range(NCH):
                    last = e.matmul(
                        ps, lw1_sb[:, kc * 512 + lc * 128:kc * 512 + lc * 128 + 128],
                        xnrf_sb[:, kc * 512:(kc + 1) * 512],
                        start=(kc == 0), stop=(kc == NCH - 1))
                return last

            OP("pe", mml1, r=["lw1_sb", "xnrf_sb"], w=[pbuf])
            OP("sc", lambda e, lc=lc, ps=ps: e.activation(
                lh_sb[:, lc * 512:(lc + 1) * 512], ps, AF.Gelu,
                bias=BT["lb1"][:, lc:lc + 1]),
               r=[pbuf, "btall"], w=["lh_sb"])

        PS4 = (ps_a, ps_b, ps_cd[:, 0:512], ps_cd[:, 512:1024])
        PB4 = ("ps_a", "ps_b", "ps_cd", "ps_cd")
        for hh2 in range(2):
            if hh2 == 1:
                for kc in range(NCH):
                    f1load(kc, 1)
            for hq in range(16):
                hc = hh2 * 16 + hq
                ps = PS4[hq % 4]
                pbuf = PB4[hq % 4]

                def mmf1(e, hq=hq, tokr=tokr, ps=ps):
                    last = None
                    for kc in range(NCH):
                        last = e.matmul(
                            ps, f1sl[:, kc * 2048 + hq * 128:kc * 2048 + hq * 128 + 128],
                            big_u[:, kc * TOK + tokr:kc * TOK + tokr + TC],
                            start=(kc == 0), stop=(kc == NCH - 1))
                    return last

                OP("pe", mmf1,
                   r=["f1sl%d" % kc3 for kc3 in range(NCH)] + ["big_u"],
                   w=[pbuf])
                OP("sc", lambda e, hc=hc, ps=ps: e.activation(
                    f1h[:, hc * 512:(hc + 1) * 512], ps, AF.Gelu,
                    bias=BT["f1b"][:, hc:hc + 1]),
                   r=[pbuf, "btall"], w=["f1h"])
            light1(hh2 * 2)
            light1(hh2 * 2 + 1)
        if ck + 1 < NCK:
            for kc in range(2, NCH):
                f1load(kc, 0, q="s2")
        for fp in range(2):
            for kc2 in range(NHC):
                sl = f2sl[:, (kc2 % 4) * 1024:(kc2 % 4) * 1024 + 1024]
                sb2 = "f2sl%d" % (kc2 % 4)
                wl = ([sb2, "f1sl0"] if (fp == 0 and kc2 < 2) else
                      [sb2, "f1sl1"] if (fp == 0 and kc2 < 4) else [sb2])
                OP("sp", lambda e, kc2=kc2, sl=sl: e.dma_start(
                    out=sl, in_=f216[kc2 * 128:(kc2 + 1) * 128, :]),
                   r=["f216"], w=wl, dma=True)

                def mmf2(e, kc2=kc2, fp=fp, sl=sl):
                    last = None
                    for q in range(4):
                        occ = fp * 4 + q
                        ps = (ps_a, ps_b, ps_cd[:, 0:512], ps_cd[:, 512:1024])[q]
                        last = e.matmul(
                            ps, sl[:, occ * 128:(occ + 1) * 128],
                            f1h[:, kc2 * 512:(kc2 + 1) * 512],
                            start=(kc2 == 0), stop=False)
                    return last

                OP("pe", mmf2, r=[sb2, "f1h"], w=["ps_a", "ps_b", "ps_cd"])
            def mmfu(e, fp=fp, tokr=tokr):
                last = None
                for q in range(4):
                    occ = fp * 4 + q
                    ps = (ps_a, ps_b, ps_cd[:, 0:512], ps_cd[:, 512:1024])[q]
                    last = e.matmul(
                        ps, idb_t, big_u[:, occ * TOK + tokr:occ * TOK + tokr + TC],
                        start=False, stop=True, skip_group_check=True)
                return last

            OP("pe", mmfu, r=["idb_t", "big_u"], w=["ps_a", "ps_b", "ps_cd"])
            for q in range(4):
                occ = fp * 4 + q
                ps = (ps_a, ps_b, ps_cd[:, 0:512], ps_cd[:, 512:1024])[q]
                pbuf = ("ps_a", "ps_b", "ps_cd", "ps_cd")[q]
                OP("sc", lambda e, occ=occ, ps=ps: e.activation(
                    uffn[:, occ * 512:(occ + 1) * 512], ps, AF.Identity,
                    bias=BT["f2b"][:, occ:occ + 1]),
                   r=[pbuf, "btall"], w=["uffn"])
        if ck + 1 < NCK:
            f1load(0, 0, q="s2")
            f1load(1, 0, q="s2")
        for kc2 in range(4):
            OP("sp", lambda e, kc2=kc2: e.dma_start(
                out=lw2sl[:, kc2 * 1024:(kc2 + 1) * 1024],
                in_=lw216[kc2 * 128:(kc2 + 1) * 128, :]),
               r=["lw216"], w=["lw2sl"], dma=True)
        OP("sp", lambda e: e.dma_start(
            out=pt_sb.rearrange("p (kc m) -> p kc m", m=1024),
            in_=p16.rearrange("(kc p) m -> p kc m", p=128)),
           r=["p16"], w=["pt_sb"], dma=True)
        for tg in range(4):
            def mmout(e, tg=tg):
                last = None
                for kc in range(NCH):
                    for half in range(2):
                        last = e.matmul(
                            ps_cd[:, half * 512:half * 512 + 512],
                            uffn[:, kc * 512 + tg * 128:kc * 512 + tg * 128 + 128],
                            pt_sb[:, kc * 1024 + half * 512:kc * 1024 + half * 512 + 512],
                            start=(kc == 0), stop=False)
                return last

            OP("pe", mmout, r=["uffn", "pt_sb"], w=["ps_cd"])

            def mmlw2(e, tg=tg):
                last = None
                for kc2 in range(4):
                    for half in range(2):
                        last = e.matmul(
                            ps_cd[:, half * 512:half * 512 + 512],
                            lh_sb[:, kc2 * 512 + tg * 128:kc2 * 512 + tg * 128 + 128],
                            lw2sl[:, kc2 * 1024 + half * 512:kc2 * 1024 + half * 512 + 512],
                            start=False, stop=False)
                return last

            OP("pe", mmlw2, r=["lh_sb", "lw2sl"], w=["ps_cd"])
            gt = (W + tokr) // 128 + tg
            xbt = xb2w[:, 0:1024]
            xbb = "xb2ws"
            OP("sp", lambda e, gt=gt, xbt=xbt: e.dma_start(
                out=xbt, in_=xo[gt * 128:(gt + 1) * 128, :]),
               w=[xbb], dma=True)

            def mmxb(e):
                e.matmul(ps_cd[:, 0:512], orowb, pbl2b[0:1, 0:512],
                         start=False, stop=True, skip_group_check=True)
                return e.matmul(ps_cd[:, 512:1024], orowb, pbl2b[0:1, 512:1024],
                                start=False, stop=True, skip_group_check=True)

            OP("pe", mmxb, r=["orowb", "pbl2b"], w=["ps_cd"])
            OP("ve", lambda e, xbt=xbt: e.tensor_tensor(ytile, ps_cd, xbt,
                                                        ALU.add),
               r=["ps_cd", xbb], w=["ytile"])
            OP("gp", lambda e, tokr=tokr, tg=tg: e.dma_start(
                out=y[tokr + tg * 128:tokr + tg * 128 + 128, :], in_=ytile),
               r=["ytile"], w=["y_d"], dma=True)

    yv = y.rearrange("(p f) d -> f p d", f=32)
    for f in range(32):
        yt = (ytile, xb2w[:, 0:1024])[f % 2]
        yb = ("ytile", "xb2ws")[f % 2]
        OP("sp", lambda e, f=f, yt=yt: e.dma_start(out=yt, in_=yv[f]),
           r=["y_d"], w=[yb], dma=True)
        OP("gp", lambda e, f=f, yt=yt: e.indirect_dma_start(
            out=yt, out_offset=None,
            in_=hvo_d, in_offset=IndirectOffsetOnAxis(offs_i[:, f:f + 1], 0),
            bounds_check=bcreg, oob_is_err=False, compute_op=ALU.add),
           r=["hvo_d", "offs_i"], w=[yb], dma=True)
        OP("gp", lambda e, f=f, yt=yt: e.dma_start(out=yv[f], in_=yt),
           r=[yb], w=["y_d"], dma=True)

    c.final_wait("gp", ["gp", "sp", "pe", "sc", "ve"])
    return nc


_CACHE = {}


def _get_nc():
    if "nc" not in _CACHE:
        _CACHE["nc"] = build()
    return _CACHE["nc"]


def kernel(**inputs):
    f32 = np.float32
    x = np.ascontiguousarray(inputs["L0_attn_emb"], dtype=f32)
    g = lambda n: np.ascontiguousarray(inputs[n], dtype=f32)

    WaT = np.ascontiguousarray(g("L0_attn_Wa_w").T)
    WbT = np.ascontiguousarray(g("L0_attn_Wb_w").T)
    WoT = np.ascontiguousarray(g("L0_attn_Wo_w").T)
    CT = np.ascontiguousarray(g("L0_attn_C_w").T)
    PT = np.ascontiguousarray(g("L0_attn_p_w").T)
    F1T = np.ascontiguousarray(g("L0_attn_f1_w").T)
    F2T = np.ascontiguousarray(g("L0_attn_f2_w").T)
    ident = np.eye(128, dtype=f32)
    identb = np.eye(128).astype(ml_dtypes.bfloat16)

    u_ix = np.arange(RT)
    p_ix = np.arange(128)
    tmap = np.where(u_ix[None, :] < 32,
                    p_ix[:, None] * 32 + u_ix[None, :],
                    4096 + p_ix[:, None] * 32 + (u_ix[None, :] - 32))
    perm = tmap.T.reshape(-1)

    shared = {
        "waT": WaT, "wbT": WbT, "woT": WoT, "cT": CT, "pTw": PT,
        "f1T": F1T, "f2T": F2T,
        "lw1": g("L0_ffn_light_w1"), "lw2": g("L0_ffn_light_w2"),
        "hw1": g("L0_ffn_heavy_w1"), "hw2": g("L0_ffn_heavy_w2"),
        "idf": ident, "idb": identb,
        "wab": g("L0_attn_Wa_b"), "wbb": g("L0_attn_Wb_b"),
        "wob": g("L0_attn_Wo_b"), "cb": g("L0_attn_C_b"),
        "lng": g("L0_attn_ln_g"), "lnb": g("L0_attn_ln_b"),
        "f2b": g("L0_attn_f2_b"), "pb": g("L0_attn_p_b"),
        "lb2": g("L0_ffn_light_b2"), "hb2": g("L0_ffn_heavy_b2"),
        "hwe": np.ascontiguousarray(np.repeat(g("L0_attn_head_weights"), 64)),
        "rtok": g("L0_ffn_routing_token"),
        "lgam": g("L0_ffn_light_gamma"), "hgam": g("L0_ffn_heavy_gamma"),
        "f1b": g("L0_attn_f1_b"), "hb1": g("L0_ffn_heavy_b1"),
        "lb1": g("L0_ffn_light_b1"),
    }
    def bt(v, n):
        return np.asarray(v, f32).reshape(n, 128).T
    btall = np.zeros((128, 160), f32)
    btall[:, 0:8] = bt(shared["wab"], 8)
    btall[:, 8:16] = bt(shared["wbb"], 8)
    btall[:, 16:24] = bt(shared["wob"], 8)
    btall[:, 24:32] = bt(shared["cb"], 8)
    btall[:, 32:40] = bt(shared["f2b"], 8)
    btall[:, 40:48] = bt(shared["pb"], 8)
    btall[:, 48:56] = bt(shared["hwe"], 8)
    btall[:, 56:64] = bt(shared["lgam"], 8)
    btall[:, 64:72] = bt(shared["hgam"], 8)
    btall[:, 72:104] = bt(shared["f1b"], 32)
    btall[:, 104:136] = bt(shared["hb1"], 32)
    btall[:, 136:140] = bt(shared["lb1"], 4)
    shared["btall"] = np.ascontiguousarray(btall)

    in_maps = []
    for core in range(8):
        b, half = core // 2, core % 2
        t0 = half * 4096
        xr = x[b]
        if half == 0:
            warm = np.zeros((W, D), f32)
            wm = np.zeros((1, W), f32)
        else:
            warm = xr[t0 - W:t0]
            wm = np.ones((1, W), f32)
        m = dict(shared)
        m["xo"] = np.ascontiguousarray(np.concatenate([warm, xr[t0:t0 + 4096]], 0))
        m["xp"] = np.ascontiguousarray(xr[perm])
        m["wmask"] = wm
        m["hs"] = np.array([[1.0 - half, float(half)]], f32)
        hsr = np.zeros((1, 128), f32)
        hsr[0, 0] = 1.0 - half
        hsr[0, 1] = float(half)
        m["hsrow"] = hsr
        in_maps.append(m)

    nc = _get_nc()
    res = run_bass_kernel_spmd(nc, in_maps, core_ids=list(range(8)))
    out = np.empty((4, 8192, 1024), f32)
    for core in range(8):
        b, half = core // 2, core % 2
        out[b, half * 4096:(half + 1) * 4096] = np.asarray(res.results[core]["y"])
    return out


# version 18
# speedup vs baseline: 1.0219x; 1.0219x over previous
import numpy as np
import ml_dtypes
import concourse.bass as bass
import concourse.mybir as mybir
from concourse.bass_utils import run_bass_kernel_spmd
from concourse.bass_core_rust import IndirectOffsetOnAxis

F32 = mybir.dt.float32
BF16 = mybir.dt.bfloat16
I32 = mybir.dt.int32
U8 = mybir.dt.uint8
AF = mybir.ActivationFunctionType
ALU = mybir.AluOpType
AX = mybir.AxisListType

D = 1024
NCH = 8
TOK = 4096
W = 512
TC = 512
NCK = TOK // TC
HID = 4096
NHC = HID // 128
LH = 512
NLC = LH // 128
NSEL = 1024
RT = 64
NTO = (W + TOK) // 128
HUGE = 100000.0
EPS = 1e-5


class Buf:
    __slots__ = ("wd", "readers")

    def __init__(self):
        self.wd = {}
        self.readers = []


class Ctx:
    ENG = {"sp": "sync", "gp": "gpsimd", "pe": "tensor", "sc": "scalar",
           "ve": "vector", "s2": "scalar"}

    DMAQ = ("sp", "gp", "s2")
    NROT = 8

    def __init__(self, nc):
        self.nc = nc
        self.sem = {}
        self.cnt = {}
        self.lastwait = {}
        self.B = {}
        self.dix = {}
        for k in self.ENG:
            self.sem[k] = nc.semaphore("s_" + k).__enter__()
            self.cnt[k] = 0
        for q in self.DMAQ:
            self.dix[q] = 0
            for j in range(self.NROT):
                self.sem[(q, j)] = nc.semaphore("d_%s%d" % (q, j)).__enter__()
                self.cnt[(q, j)] = 0

    def eng(self, k):
        return getattr(self.nc, self.ENG[k])

    def buf(self, name):
        if name not in self.B:
            self.B[name] = Buf()
        return self.B[name]

    def _wait(self, ek, tick):
        dk, val, is_dma = tick
        if ek == dk == "pe" and not is_dma:
            return
        if self.lastwait.get((ek, dk), 0) >= val:
            return
        self.eng(ek).wait_ge(self.sem[dk], val)
        self.lastwait[(ek, dk)] = val

    def op(self, ek, emit, r=(), w=(), dma=False):
        reads = [self.buf(x) for x in r]
        writes = [self.buf(x) for x in w]
        for b in reads:
            for dk, val in b.wd.items():
                self._wait(ek, (dk, val, not isinstance(dk, str)))
        for b in writes:
            for dk, val in b.wd.items():
                self._wait(ek, (dk, val, not isinstance(dk, str)))
            for t in b.readers:
                self._wait(ek, t)
        ins = emit(self.eng(ek))
        if dma:
            sk = (ek, self.dix[ek] % self.NROT)
            self.dix[ek] += 1
            self.cnt[sk] += 16
            ins.then_inc(self.sem[sk], 16)
            t = (sk, self.cnt[sk], True)
        else:
            self.cnt[ek] += 1
            ins.then_inc(self.sem[ek], 1)
            t = (ek, self.cnt[ek], False)
        for b in reads:
            b.readers.append(t)
        for b in writes:
            b.wd[t[0]] = t[1]
            b.readers = []
        return t

    def final_wait(self, ek, dks=None):
        for dk, n in self.cnt.items():
            if n > 0:
                self.eng(ek).wait_ge(self.sem[dk], n)

    def fence(self):
        for q in self.DMAQ:
            for j in range(self.NROT):
                sk = (q, j)
                if self.cnt[sk] > self.lastwait.get((q, sk), 0):
                    self.eng(q).wait_ge(self.sem[sk], self.cnt[sk])
                    self.lastwait[(q, sk)] = self.cnt[sk]
        self.nc.all_engine_barrier()
        for ek in self.ENG:
            for dk in self.cnt:
                self.lastwait[(ek, dk)] = max(
                    self.lastwait.get((ek, dk), 0), self.cnt[dk])


BISECT_FOLDS = True


def build(dbg=False):
    nc = bass.Bass("TRN2", target_bir_lowering=False, debug=False)
    c = Ctx(nc)
    OP = c.op
    bcreg = nc.gpsimd.alloc_register("bcreg")
    nc.gpsimd.reg_mov(bcreg, NSEL - 1)

    def din(name, shape, dt=F32):
        return nc.dram_tensor(name, shape, dt, kind="ExternalInput").ap()

    xo = din("xo", [W + TOK, D])
    xp = din("xp", [RT * 128, D])
    wmask_i = din("wmask", [1, W])
    hs_i = din("hs", [1, 2])
    idf_i = din("idf", [128, 128])
    idb_i = din("idb", [128, 128], BF16)
    waT = din("waT", [D, D]); wbT = din("wbT", [D, D]); woT = din("woT", [D, D])
    cT = din("cT", [D, D]); pTw = din("pTw", [D, D])
    f1T = din("f1T", [D, HID]); f2T = din("f2T", [HID, D])
    lw1 = din("lw1", [D, LH]); lw2 = din("lw2", [LH, D])
    hw1 = din("hw1", [D, HID]); hw2 = din("hw2", [HID, D])
    bn = ["wab", "wbb", "wob", "cb", "lng", "lnb", "f2b", "pb", "lb2",
          "hb2", "hwe", "rtok", "lgam", "hgam"]
    b_in = {n: din(n, [D]) for n in bn}
    btall_i = din("btall", [128, 160])
    f1b_i = din("f1b", [HID]); hb1_i = din("hb1", [HID]); lb1_i = din("lb1", [LH])
    hsrow_i = din("hsrow", [1, 128])

    y = nc.dram_tensor("y", [TOK, D], F32, kind="ExternalOutput").ap()

    dk = "ExternalOutput" if dbg else "Internal"

    def dsc(name, shape, dt=BF16, kind="Internal"):
        return nc.dram_tensor(name, shape, dt, kind=kind).ap()

    wa16 = dsc("wa16", [D, D]); wb16 = dsc("wb16", [D, D]); wo16 = dsc("wo16", [D, D])
    c16 = dsc("c16", [D, D]); p16 = dsc("p16", [D, D])
    f116 = dsc("f116", [D, HID]); f216 = dsc("f216", [HID, D])
    lw116 = dsc("lw116", [D, LH]); lw216 = dsc("lw216", [LH, D])
    hw116 = dsc("hw116", [D, HID]); hw216 = dsc("hw216", [HID, D])
    xnrtm_d = dsc("xnrtm_d", [TOK, D], BF16, dk)
    xnrfm_d = dsc("xnrfm_d", [NCH, 128, TOK], BF16, dk)
    wpfm_d = dsc("wpfm_d", [NCH, 128, TOK], BF16, dk)
    comp_d = dsc("comp_d", [NSEL, D], BF16, dk)
    offs_flat_d = dsc("offs_flat_d", [TOK], mybir.dt.int32, "Internal")
    hvo_d = dsc("hvo_d", [NSEL, D], F32, dk)
    if dbg:
        sdbg_d = dsc("sdbg_d", [128, 5 * RT], F32, dk)
        hdbg_d = dsc("hdbg_d", [NCH, 128, TOK], BF16, dk)
        udbg_d = dsc("udbg_d", [NCH, 128, TOK], BF16, dk)
        ahist_d = dsc("ahist_d", [1, 64], F32, dk)
        zdbg_d = dsc("zdbg_d", [128, NCH * 512], F32, dk)
        wraw_d = dsc("wraw_d", [128, 512], F32, dk)
        btdbg_d = dsc("btdbg_d", [128, 16], F32, dk)
        wpsb_d = dsc("wpsb_d", [128, NCH * 512], BF16, dk)
        prowdbg_d = dsc("prowdbg_d", [1, 1536], F32, dk)
        st36_d = dsc("st36_d", [128, 8 * NTO], F32, dk)
        rwkdbg_d = dsc("rwkdbg_d", [128, 3 * RT], F32, dk)
        xndbg_d = dsc("xndbg_d", [W + TOK, D], BF16, dk)
        sclcbs_d = dsc("sclcbs_d", [128, 16], F32, dk)

    def alloc(name, shape, dt):
        return nc.alloc_sbuf_tensor(name, shape, dt).ap()

    rtn = ["s_pm", "sb_pm", "u_pm", "sat", "scores", "wk1", "wk2", "wk3",
           "wk4", "incl", "rank", "offsf", "selt", "su", "ones64"]
    offs_i = alloc("offs_i", [128, 32], I32)
    offstm_i = alloc("offstm_i", [128, 32], I32)
    scl = alloc("scl", [1, 64], F32)
    orow = alloc("orow", [1, 128], F32)
    onesq = alloc("onesq", [128, 128], F32)
    srow = alloc("srow", [1, 128], F32)
    cbs = alloc("cbs", [128, 8], F32)
    carry = alloc("carry", [128, NCH], F32)
    btall = alloc("btall_sb", [128, 160], F32)
    _btc = {"wab": (0, 8), "wbb": (8, 16), "wocb": (16, 24), "tmp8": (24, 32),
            "f2b": (32, 40), "pb": (40, 48), "hwe": (48, 56), "lgam": (56, 64),
            "hgam": (64, 72), "f1b": (72, 104), "hb1": (104, 136),
            "lb1": (136, 140), "wocb2": (140, 148)}
    BT = {n: btall[:, a:b] for n, (a, b) in _btc.items()}
    idf_t = alloc("idf_t", [128, 128], F32)
    idb_t = alloc("idb_t", [128, 128], BF16)
    hsb = alloc("hsb", [128, 2], F32)
    ones1 = alloc("ones1", [128, 1], F32)
    epscol = alloc("epscol", [128, 1], F32)
    lngrow = alloc("lngrow", [1, D], F32)
    lnbrow = alloc("lnbrow", [1, D], F32)
    lngrowb = alloc("lngrowb", [1, D], BF16)
    lnbrowb = alloc("lnbrowb", [1, D], BF16)
    orowb = alloc("orowb", [1, 128], BF16)
    ones1b = alloc("ones1b", [128, 1], BF16)

    AR_W = nc.alloc_sbuf_tensor("ar_w", [128, 49152], U8)
    AR_H = nc.alloc_sbuf_tensor("ar_h", [128, 65536], U8)
    AR_U = nc.alloc_sbuf_tensor("ar_u", [128, 65536], U8)
    AR_T = nc.alloc_sbuf_tensor("ar_t", [128, 16384], U8)

    def sub(arena, name, shape, dt, off):
        addr = nc.lookup_mloc(arena).addr + off
        return nc.alloc_sbuf_tensor_at(name, shape, dt, offset=addr).ap()

    wa_sb = sub(AR_W, "wa_sb", [128, NCH * 1024], BF16, 0)
    wb_sb = sub(AR_W, "wb_sb", [128, NCH * 1024], BF16, 16384)
    wo_sb = sub(AR_W, "wo_sb", [128, NCH * 1024], BF16, 32768)
    compfm = sub(AR_W, "compfm", [128, NCH * 1024], BF16, 0)
    hw1sl = sub(AR_W, "hw1sl", [128, NCH * 2048], BF16, 16384)
    hw2sl = sub(AR_W, "hw2sl", [128, 4 * 1024], BF16, 0)
    hvot = sub(AR_W, "hvot", [128, 2 * 1024], F32, 8192)
    hb2b = sub(AR_W, "hb2b", [1, D], BF16, 16384)
    cs_sb = sub(AR_W, "cs_sb", [128, NCH * 1024], BF16, 0)
    z_sb = sub(AR_W, "z_sb", [128, NCH * 512], F32, 16384)
    wp_sb = sub(AR_W, "wp_sb", [128, 2 * NCH * 512], BF16, 32768)
    pt_sb = sub(AR_W, "pt_sb", [128, NCH * 1024], BF16, 0)
    lw1_sb = sub(AR_W, "lw1_sb", [128, NCH * 512], BF16, 16384)
    lw2sl = sub(AR_W, "lw2sl", [128, 4 * 1024], BF16, 24576)
    xnrf_sb = sub(AR_W, "xnrf_sb", [128, NCH * 512], BF16, 32768)
    lh_sb = sub(AR_W, "lh_sb", [128, NLC * 512], BF16, 40960)
    xb2w = sub(AR_W, "xb2w", [128, 1024], F32, 45056)

    h_res = sub(AR_H, "h_res", [128, NCH * TOK], BF16, 0)
    f1h = sub(AR_H, "f1h", [128, NHC * 512], BF16, 0)
    f1sl = sub(AR_H, "f1sl", [128, NCH * 2048], BF16, 32768)
    f2sl = sub(AR_H, "f2sl", [128, 4 * 1024], BF16, 32768)

    big_u = sub(AR_U, "big_u", [128, NCH * TOK], BF16, 0)
    xbuf = sub(AR_U, "xbuf", [128, 2 * 1024], F32, 0)
    xbb0 = sub(AR_U, "xbb0", [128, 1024], BF16, 60672)
    xbb1 = sub(AR_U, "xbb1", [128, 1024], BF16, 62720)
    xntm = sub(AR_U, "xntm", [128, 1024], BF16, 8192)
    xnrtm = sub(AR_U, "xnrtm", [128, 1024], BF16, 10240)
    xnfm = sub(AR_U, "xnfm", [128, NCH * 512], BF16, 12288)
    xnrfm = sub(AR_U, "xnrfm", [128, NCH * 512], BF16, 20480)
    acc_s = sub(AR_U, "acc_s", [128, 512], F32, 28672)
    bcc_s = sub(AR_U, "bcc_s", [128, 512], F32, 30720)
    hcc_s = sub(AR_U, "hcc_s", [128, 512], F32, 32768)
    wpout = sub(AR_U, "wpout", [128, 2 * 512], BF16, 34816)
    sbuf2 = sub(AR_U, "sbuf2", [128, 2 * 1024], F32, 8192)
    gsl32 = sub(AR_U, "gsl32", [128, HID], F32, 16384)
    gsl16 = sub(AR_U, "gsl16", [128, HID], BF16, 32768)
    rbc = sub(AR_U, "rbc", [128, 1024], F32, 40960)
    rtrow = sub(AR_U, "rtrow", [1, 1024], F32, 45056)
    wmbc = sub(AR_U, "wmbc", [128, 512], F32, 49152)
    wmrow = sub(AR_U, "wmrow", [1, 512], F32, 51200)
    xr_s = sub(AR_U, "xr_s", [128, 1024], F32, 49152)
    scdump = sub(AR_U, "scdump", [128, 1024], F32, 60672)
    comptm = sub(AR_U, "comptm", [128, 2 * 1024], BF16, 0)
    RTT = {n: sub(AR_U, "rt_" + n, [128, RT], F32, 53248 + i * 256)
           for i, n in enumerate(rtn)}
    satu8 = sub(AR_U, "satu8", [128, RT], U8, 53248 + 15 * 256)
    statS = sub(AR_U, "statS", [128, NTO], F32, 57728)
    statQ = sub(AR_U, "statQ", [128, NTO], F32, 57888)
    tmp36a = sub(AR_U, "tmp36a", [128, NTO], F32, 58048)
    tmp36b = sub(AR_U, "tmp36b", [128, NTO], F32, 58208)
    muall = sub(AR_U, "muall", [128, NTO], F32, 58368)
    rstdall = sub(AR_U, "rstdall", [128, NTO], F32, 58528)
    rrall = sub(AR_U, "rrall", [128, NTO], F32, 58688)
    on2 = sub(AR_U, "on2", [2, 128], F32, 58880)
    ahist = sub(AR_U, "ahist", [1, 64], F32, 60416)
    bt2 = sub(AR_U, "bt2", [2, 128], F32, 59392)
    bs2 = sub(AR_U, "bs2", [2, 128], F32, 59904)

    uffn = sub(AR_T, "uffn", [128, NCH * 512], BF16, 0)
    ytile = sub(AR_T, "ytile", [128, 1024], F32, 8192)
    pbl2b = sub(AR_T, "pbl2b", [1, D], BF16, 12288)
    prow = sub(AR_T, "prow", [1, 4 * 512], F32, 0)
    prowb = sub(AR_T, "prowb", [1, 3 * 512], BF16, 8192)
    zb16 = sub(AR_T, "zb16", [128, 1024], BF16, 11264)
    zsqb = sub(AR_T, "zsqb", [128, 1024], BF16, 13312)

    ps_a = nc.alloc_psum_tensor("ps_a", [128, 512], F32).ap()
    ps_b = nc.alloc_psum_tensor("ps_b", [128, 512], F32).ap()
    ps_cd = nc.alloc_psum_tensor("ps_cd", [128, 1024], F32).ap()
    ps_tr = nc.alloc_psum_tensor("ps_tr", [128, 1024], BF16).ap()
    ps_s1 = nc.alloc_psum_tensor("ps_s1", [128, 512], F32).ap()
    ps_s2 = nc.alloc_psum_tensor("ps_s2", [1, 512], F32).ap()
    ps_x = nc.alloc_psum_tensor("ps_x", [128, 512], F32).ap()

    for nm, src, dst in [("wa16", waT, wa16), ("wb16", wbT, wb16),
                         ("wo16", woT, wo16), ("c16", cT, c16)]:
        OP("gp", lambda e, s=src, d=dst: e.dma_start(out=d, in_=s),
           w=[nm], dma=True)
    late_casts = [("f116", f1T, f116), ("f216", f2T, f216),
                  ("lw216", lw2, lw216), ("hw216", hw2, hw216),
                  ("p16", pTw, p16)]

    OP("sp", lambda e: e.dma_start(out=idf_t, in_=idf_i), w=["idf_t"], dma=True)
    OP("sp", lambda e: e.dma_start(out=idb_t, in_=idb_i), w=["idb_t"], dma=True)
    OP("ve", lambda e: e.tensor_scalar(orow, idf_t[0:1, :], 0.0, 1.0,
                                       ALU.mult, ALU.add),
       r=["idf_t"], w=["orow"])
    OP("ve", lambda e: e.tensor_scalar(onesq, idf_t, 0.0, 1.0,
                                       ALU.mult, ALU.add),
       r=["idf_t"], w=["onesq"])
    OP("ve", lambda e: e.tensor_scalar(ones1, idf_t[:, 0:1], 0.0, 1.0,
                                       ALU.mult, ALU.add),
       r=["idf_t"], w=["ones1"])
    OP("ve", lambda e: e.tensor_scalar(orowb, idf_t[0:1, :], 0.0, 1.0,
                                       ALU.mult, ALU.add),
       r=["idf_t"], w=["orowb"])
    OP("ve", lambda e: e.tensor_scalar(ones1b, idf_t[:, 0:1], 0.0, 1.0,
                                       ALU.mult, ALU.add),
       r=["idf_t"], w=["ones1b"])
    OP("ve", lambda e: e.tensor_scalar(epscol, idf_t[:, 0:1], 0.0, EPS,
                                       ALU.mult, ALU.add),
       r=["idf_t"], w=["epscol"])

    OP("sp", lambda e: e.dma_start(out=btall[:, 0:160], in_=btall_i),
       w=["btall"], dma=True)
    OP("ve", lambda e: e.tensor_tensor(BT["wocb2"], BT["wocb"], BT["tmp8"],
                                       ALU.add),
       r=["btall"], w=["btall"])

    OP("sp", lambda e: e.dma_start(out=lngrow, in_=b_in["lng"].rearrange("(o d) -> o d", o=1)),
       w=["lngrow"], dma=True)
    OP("sp", lambda e: e.dma_start(out=lnbrow, in_=b_in["lnb"].rearrange("(o d) -> o d", o=1)),
       w=["lnbrow"], dma=True)
    OP("sc", lambda e: e.copy(lngrowb, lngrow), r=["lngrow"], w=["lngrowb"])
    OP("sc", lambda e: e.copy(lnbrowb, lnbrow), r=["lnbrow"], w=["lnbrowb"])
    OP("sp", lambda e: e.dma_start(out=rtrow, in_=b_in["rtok"].rearrange("(o d) -> o d", o=1)),
       w=["rtrow"], dma=True)

    OP("sp", lambda e: e.dma_start(out=srow, in_=hsrow_i), w=["srow"], dma=True)
    OP("pe", lambda e: e.matmul(ps_x[:, 0:2], orow, srow[0:1, 0:2],
                                start=True, stop=True),
       r=["orow", "srow"], w=["ps_x"])
    OP("ve", lambda e: e.tensor_copy(hsb, ps_x[:, 0:2]), r=["ps_x"], w=["hsb"])


    for hf in range(2):
        OP("pe", lambda e, hf=hf: e.matmul(
            ps_cd[:, hf * 512:(hf + 1) * 512], orow,
            rtrow[0:1, hf * 512:(hf + 1) * 512], start=True, stop=True),
           r=["orow", "rtrow"], w=["ps_cd"])
    OP("ve", lambda e: e.tensor_copy(rbc[:, 0:512], ps_cd[:, 0:512]),
       r=["ps_cd"], w=["rbc"])
    OP("ve", lambda e: e.tensor_copy(rbc[:, 512:1024], ps_cd[:, 512:1024]),
       r=["ps_cd"], w=["rbc"])
    OP("ve", lambda e: e.reduce_sum(scl[0:1, 0:1], rtrow, axis=AX.X),
       r=["rtrow"], w=["scl"])
    OP("pe", lambda e: e.matmul(ps_x[:, 0:1], orow, scl[0:1, 0:1],
                                start=True, stop=True),
       r=["orow", "scl"], w=["ps_x"])
    OP("ve", lambda e: e.tensor_copy(cbs[:, 2:3], ps_x[:, 0:1]),
       r=["ps_x"], w=["cbs"])

    def fold_step(kc, heavy):
        if heavy:
            OP("sp", lambda e, kc=kc: e.dma_start(
                out=gsl32, in_=hw1[kc * 128:(kc + 1) * 128, :]),
                w=["gsl32"], dma=True)
            OP("ve", lambda e, kc=kc: e.tensor_scalar(
                gsl16, gsl32, BT["hgam"][:, kc:kc + 1], None, ALU.mult),
                r=["gsl32", "btall"], w=["gsl16"])
            OP("gp", lambda e, kc=kc: e.dma_start(
                out=hw116[kc * 128:(kc + 1) * 128, :], in_=gsl16),
                r=["gsl16"], w=["hw116"], dma=True)
        else:
            OP("sp", lambda e, kc=kc: e.dma_start(
                out=gsl32[:, 0:LH], in_=lw1[kc * 128:(kc + 1) * 128, :]),
                w=["gsl32"], dma=True)
            OP("ve", lambda e, kc=kc: e.tensor_scalar(
                gsl16[:, 0:LH], gsl32[:, 0:LH], BT["lgam"][:, kc:kc + 1], None,
                ALU.mult),
                r=["gsl32", "btall"], w=["gsl16"])
            OP("gp", lambda e, kc=kc: e.dma_start(
                out=lw116[kc * 128:(kc + 1) * 128, :], in_=gsl16[:, 0:LH]),
                r=["gsl16"], w=["lw116"], dma=True)
    fold_items = [(kc, True) for kc in range(NCH)] + \
                 [(kc, False) for kc in range(NCH)]

    def preduce(src_ap, src_buf, dst_slot, op=ALU.add, colslot=0):
        red = (lambda e: e.reduce_sum(cbs[:, colslot:colslot + 1], src_ap, axis=AX.X)) \
            if op == ALU.add else \
            (lambda e: e.tensor_reduce(cbs[:, colslot:colslot + 1], src_ap,
                                       axis=AX.X, op=op))
        OP("ve", red, r=[src_buf], w=["cbs"])
        OP("pe", lambda e: e.transpose(ps_s1[0:1, 0:128],
                                       cbs[:, colslot:colslot + 1], idf_t),
           r=["cbs", "idf_t"], w=["ps_s1"])
        OP("ve", lambda e: e.tensor_copy(srow, ps_s1[0:1, 0:128]),
           r=["ps_s1"], w=["srow"])
        red2 = (lambda e: e.reduce_sum(scl[0:1, dst_slot:dst_slot + 1], srow, axis=AX.X)) \
            if op == ALU.add else \
            (lambda e: e.tensor_reduce(scl[0:1, dst_slot:dst_slot + 1], srow,
                                       axis=AX.X, op=op))
        OP("ve", red2, r=["srow"], w=["scl"])

    def bcast(slot, col):
        OP("pe", lambda e: e.matmul(ps_s1[:, 0:1], orow,
                                    scl[0:1, slot:slot + 1], start=True, stop=True),
           r=["orow", "scl"], w=["ps_s1"])
        OP("ve", lambda e: e.tensor_copy(cbs[:, col:col + 1], ps_s1[:, 0:1]),
           r=["ps_s1"], w=["cbs"])

    def s_load(u):
        st = sbuf2[:, (u % 2) * 1024:(u % 2) * 1024 + 1024]
        OP("s2", lambda e, u=u, st=st: e.dma_start(
            out=st, in_=xo[u * 128:(u + 1) * 128, :]),
           w=["sbuf2%d" % (u % 2)], dma=True)

    s_load(0)
    s_load(1)
    for u in range(RT):
        xt = xbuf[:, (u % 2) * 1024:(u % 2) * 1024 + 1024]
        xb = "xbuf%d" % (u % 2)
        OP("sp", lambda e, u=u, xt=xt: e.dma_start(
            out=xt, in_=xp[u * 128:(u + 1) * 128, :]), w=[xb], dma=True)
        OP("sc", lambda e, xt=xt, u=u: e.activation(
            scdump, xt, AF.Copy, accum_out=RTT["wk1"][:, u:u + 1]),
           r=[xb], w=["dmyA%d" % u, "rt_wk1"])
        OP("sc", lambda e, xt=xt, u=u: e.activation(
            scdump, xt, AF.Square, accum_out=RTT["wk2"][:, u:u + 1]),
           r=[xb], w=["dmyB%d" % u, "rt_wk2"])
        OP("ve", lambda e, xt=xt: e.tensor_tensor(xr_s, xt, rbc, ALU.mult),
           r=[xb, "rbc"], w=["xr_s"])
        OP("ve", lambda e, u=u: e.reduce_sum(RTT["wk3"][:, u:u + 1], xr_s,
                                             axis=AX.X),
           r=["xr_s"], w=["rt_wk3"])
        if u < NTO:
            st = sbuf2[:, (u % 2) * 1024:(u % 2) * 1024 + 1024]
            sb3 = "sbuf2%d" % (u % 2)
            OP("sc", lambda e, st=st, u=u: e.activation(
                scdump, st, AF.Copy, accum_out=statS[:, u:u + 1]),
               r=[sb3], w=["dmyC%d" % u, "statS"])
            OP("sc", lambda e, st=st, u=u: e.activation(
                scdump, st, AF.Square, accum_out=statQ[:, u:u + 1]),
               r=[sb3], w=["dmyD%d" % u, "statQ"])
            if u + 2 < NTO:
                s_load(u + 2)
        if BISECT_FOLDS and u % 4 == 1 and u // 4 < len(fold_items):
            fold_step(*fold_items[u // 4])
        if u % 8 == 7 and (u // 8 - 3) >= 0 and (u // 8 - 3) < len(late_casts):
            nm4, s4, d4 = late_casts[u // 8 - 3]
            OP("gp", lambda e, s4=s4, d4=d4: e.dma_start(out=d4, in_=s4),
               w=[nm4], dma=True)
    if not BISECT_FOLDS:
        for fi in fold_items:
            fold_step(*fi)
    if dbg:
        for i5, nm5 in enumerate(["wk1", "wk2", "wk3"]):
            OP("gp", lambda e, i5=i5, nm5=nm5: e.dma_start(
                out=rwkdbg_d[:, i5 * RT:(i5 + 1) * RT], in_=RTT[nm5]),
               r=["rt_" + nm5], w=["rwkd"], dma=True)

    OP("ve", lambda e: e.tensor_scalar(RTT["wk1"], RTT["wk1"], 1.0 / D, None,
                                       ALU.mult), r=[], w=["rt_wk1"])
    OP("ve", lambda e: e.tensor_scalar(RTT["wk2"], RTT["wk2"], 1.0 / D, None,
                                       ALU.mult), r=[], w=["rt_wk2"])
    OP("ve", lambda e: e.tensor_tensor(RTT["wk4"], RTT["wk1"], RTT["wk1"],
                                       ALU.mult), r=["rt_wk1"], w=["rt_wk4"])
    OP("ve", lambda e: e.tensor_tensor(RTT["wk2"], RTT["wk2"], RTT["wk4"],
                                       ALU.subtract), r=["rt_wk4"], w=["rt_wk2"])
    OP("sc", lambda e: e.activation(RTT["wk2"], RTT["wk2"], AF.Ln, bias=epscol),
       r=["epscol"], w=["rt_wk2"])
    OP("sc", lambda e: e.activation(RTT["wk2"], RTT["wk2"], AF.Exp, scale=-0.5),
       r=[], w=["rt_wk2"])
    OP("ve", lambda e: e.tensor_scalar(RTT["wk4"], RTT["wk1"], cbs[:, 2:3],
                                       None, ALU.mult),
       r=["rt_wk1", "cbs"], w=["rt_wk4"])
    OP("ve", lambda e: e.tensor_tensor(RTT["wk3"], RTT["wk3"], RTT["wk4"],
                                       ALU.subtract), r=["rt_wk4"], w=["rt_wk3"])
    OP("ve", lambda e: e.tensor_tensor(RTT["s_pm"], RTT["wk3"], RTT["wk2"],
                                       ALU.mult), r=["rt_wk3", "rt_wk2"],
       w=["rt_s_pm"])

    OP("sc", lambda e: e.activation(RTT["scores"], RTT["s_pm"], AF.Exp),
       r=["rt_s_pm"], w=["rt_scores"])

    OP("ve", lambda e: e.tensor_scalar(gsl16[:, 0:1024], rbc, 0.0, None,
                                       ALU.mult), r=["rbc"], w=["gsl16"])
    for g in range(8):
        OP("gp", lambda e, g=g: e.dma_start(
            out=comp_d[g * 128:(g + 1) * 128, :], in_=gsl16[:, 0:1024]),
           r=["gsl16"], w=["comp_d"], dma=True)

    SELQ = []

    def OP(ek, emit, r=(), w=(), dma=False):
        SELQ.append((ek, emit, r, w, dma))

    OP("ve", lambda e: e.tensor_scalar(RTT["ones64"], RTT["s_pm"], 0.0, 1.0,
                                       ALU.mult, ALU.add),
       r=["rt_s_pm"], w=["rt_ones64"])
    OP("ve", lambda e: e.tensor_scalar(on2, idf_t[0:2, :], 0.0, 1.0,
                                       ALU.mult, ALU.add),
       r=["idf_t"], w=["on2"])

    OP("ve", lambda e: e.tensor_scalar(RTT["sb_pm"], RTT["s_pm"], 0.0, 1.0,
                                       ALU.mult, ALU.add),
       r=["rt_s_pm"], w=["rt_sb_pm"])
    LOGK = float(np.log(np.float32(1152.0), dtype=np.float32))
    INVK = float(np.float32(1.0) / np.float32(1152.0))
    for it in range(50):
        OP("ve", lambda e: e.reduce_sum(cbs[:, 0:1], RTT["sb_pm"], axis=AX.X),
           r=["rt_sb_pm"], w=["cbs"])
        OP("pe", lambda e: e.matmul(ps_s1[:, 1:2], onesq, cbs[:, 0:1],
                                    start=True, stop=True),
           r=["cbs", "onesq"], w=["ps_s1"])
        if it < 49:
            OP("ve", lambda e: e.tensor_scalar(cbs[:, 1:2], ps_s1[:, 1:2],
                                               INVK, None, ALU.mult),
               r=["ps_s1"], w=["cbs"])
            OP("ve", lambda e: e.tensor_scalar(RTT["sb_pm"], RTT["scores"],
                                               cbs[:, 1:2], None, ALU.min),
               r=["rt_scores", "cbs"], w=["rt_sb_pm"])
    OP("ve", lambda e: e.tensor_copy(scl[0:1, 0:1], ps_s1[0:1, 1:2]),
       r=["ps_s1"], w=["scl"])
    OP("sc", lambda e: e.activation(scl[0:1, 1:2], scl[0:1, 0:1], AF.Ln),
       r=["scl"], w=["scl"])
    OP("ve", lambda e: e.tensor_scalar(scl[0:1, 3:4], scl[0:1, 1:2], LOGK,
                                       -1.0, ALU.subtract, ALU.mult),
       r=[], w=["scl"])
    bcast(3, 1)
    OP("ve", lambda e: e.tensor_scalar(RTT["u_pm"], RTT["s_pm"], cbs[:, 1:2],
                                       None, ALU.add),
       r=["rt_s_pm", "cbs"], w=["rt_u_pm"])

    OP("ve", lambda e: e.tensor_scalar(RTT["sat"], RTT["u_pm"], 0.0, None,
                                       ALU.is_ge), r=["rt_u_pm"], w=["rt_sat"])
    OP("ve", lambda e: e.tensor_scalar(RTT["wk1"], RTT["u_pm"], 0.0, None,
                                       ALU.min), r=["rt_u_pm"], w=["rt_wk1"])
    OP("sc", lambda e: e.activation(RTT["scores"], RTT["wk1"], AF.Exp),
       r=["rt_wk1"], w=["rt_scores"])
    OP("ve", lambda e: e.reduce_sum(cbs[:, 0:1], RTT["sat"], axis=AX.X),
       r=["rt_sat"], w=["cbs"])
    OP("pe", lambda e: e.matmul(ps_s1[:, 1:2], onesq, cbs[:, 0:1],
                                start=True, stop=True),
       r=["cbs", "onesq"], w=["ps_s1"])
    OP("ve", lambda e: e.tensor_scalar(cbs[:, 3:4], ps_s1[:, 1:2], -1024.0,
                                       None, ALU.add), r=["ps_s1"], w=["cbs"])
    OP("ve", lambda e: e.tensor_scalar(cbs[:, 3:4], cbs[:, 3:4], 0.0,
                                       None, ALU.max), r=[], w=["cbs"])
    OP("ve", lambda e: e.tensor_tensor(cbs[:, 3:4], ps_s1[:, 1:2],
                                       cbs[:, 3:4], ALU.subtract),
       r=["ps_s1"], w=["cbs"])
    OP("ve", lambda e: e.tensor_scalar(cbs[:, 2:3], cbs[:, 3:4], -1.0, 1024.0,
                                       ALU.mult, ALU.add),
       r=[], w=["cbs"])

    def prefix2(src_ap, src_buf, dst_ap, dst_buf, glob):
        for hf in range(2):
            sl = slice(hf * 32, (hf + 1) * 32)
            OP("ve", lambda e, sl=sl: e.tensor_tensor_scan(
                RTT["incl"][:, sl], RTT["ones64"][:, sl], src_ap[:, sl], 0.0,
                ALU.mult, ALU.add), r=["rt_ones64", src_buf], w=["rt_incl"])
        OP("ve", lambda e: e.tensor_tensor(dst_ap, RTT["incl"], src_ap,
                                           ALU.subtract),
           r=["rt_incl", src_buf], w=[dst_buf])
        OP("ve", lambda e: e.tensor_copy(cbs[:, 5:6], RTT["incl"][:, 31:32]),
           r=["rt_incl"], w=["cbs"])
        OP("ve", lambda e: e.tensor_copy(cbs[:, 6:7], RTT["incl"][:, 63:64]),
           r=["rt_incl"], w=["cbs"])
        OP("pe", lambda e: e.transpose(ps_s1[0:2, 0:128], cbs[:, 5:7],
                                       idf_t),
           r=["cbs", "idf_t"], w=["ps_s1"])
        OP("ve", lambda e: e.tensor_copy(bt2, ps_s1[0:2, 0:128]),
           r=["ps_s1"], w=["bt2"])
        OP("ve", lambda e: e.tensor_tensor_scan(
            bs2, on2, bt2, 0.0, ALU.mult, ALU.add),
           r=["on2", "bt2"], w=["bs2"])
        OP("ve", lambda e: e.tensor_tensor(bs2, bs2, bt2, ALU.subtract),
           r=["bt2"], w=["bs2"])
        if glob:
            preduce(cbs[:, 5:6], "cbs", 8, colslot=7)
            bcast(8, 7)
        OP("pe", lambda e: e.transpose(ps_s1[:, 0:2],
                                       bs2, idf_t[0:2, 0:2]),
           r=["bs2", "idf_t"], w=["ps_s1"])
        OP("ve", lambda e: e.tensor_copy(cbs[:, 5:7], ps_s1[:, 0:2]),
           r=["ps_s1"], w=["cbs"])
        OP("ve", lambda e: e.tensor_scalar(dst_ap[:, 0:32], dst_ap[:, 0:32],
                                           cbs[:, 5:6], None, ALU.add),
           r=["cbs"], w=[dst_buf])
        if glob:
            OP("ve", lambda e: e.tensor_scalar(
                dst_ap[:, 32:64], dst_ap[:, 32:64], cbs[:, 6:7], cbs[:, 7:8],
                ALU.add, ALU.add), r=["cbs"], w=[dst_buf])
        else:
            OP("ve", lambda e: e.tensor_scalar(dst_ap[:, 32:64], dst_ap[:, 32:64],
                                               cbs[:, 6:7], None, ALU.add),
               r=["cbs"], w=[dst_buf])

    prefix2(RTT["sat"], "rt_sat", RTT["rank"], "rt_rank", glob=True)
    OP("ve", lambda e: e.tensor_scalar(RTT["wk1"], RTT["rank"], cbs[:, 3:4],
                                       None, ALU.is_lt),
       r=["rt_rank", "cbs"], w=["rt_wk1"])
    OP("ve", lambda e: e.tensor_tensor(RTT["selt"], RTT["sat"], RTT["wk1"],
                                       ALU.mult),
       r=["rt_sat", "rt_wk1"], w=["rt_selt"])

    OP("ve", lambda e: e.tensor_scalar(satu8, RTT["u_pm"], 0.0, None, ALU.is_ge),
       r=["rt_u_pm"], w=["satu8"])
    OP("ve", lambda e: e.tensor_scalar(RTT["wk2"], RTT["sat"], -1.0, None,
                                       ALU.mult), r=["rt_sat"], w=["rt_wk2"])
    OP("ve", lambda e: e.select(RTT["su"], satu8, RTT["wk2"], RTT["scores"]),
       r=["satu8", "rt_wk2", "rt_scores"], w=["rt_su"])
    wk4c = RTT["wk4"][:, 0:1]
    OP("ve", lambda e: e.tensor_scalar(cbs[:, 5:6], cbs[:, 2:3], 0.0,
                                       None, ALU.mult), r=[], w=["cbs"])
    OP("ve", lambda e: e.tensor_scalar(cbs[:, 6:7], cbs[:, 2:3], 0.0,
                                       1.5, ALU.mult, ALU.add), r=[], w=["cbs"])
    for it in range(35):
        OP("ve", lambda e: e.tensor_scalar(cbs[:, 4:5], cbs[:, 5:6],
                                           cbs[:, 6:7], 0.5, ALU.add, ALU.mult),
           r=[], w=["cbs"])
        OP("ve", lambda e: e.tensor_scalar(RTT["wk1"], RTT["su"], cbs[:, 4:5],
                                           None, ALU.is_ge),
           r=["rt_su", "cbs"], w=["rt_wk1"])
        OP("ve", lambda e: e.reduce_sum(cbs[:, 0:1], RTT["wk1"], axis=AX.X),
           r=["rt_wk1"], w=["cbs"])
        OP("pe", lambda e: e.matmul(ps_s1[:, 2:3], onesq, cbs[:, 0:1],
                                    start=True, stop=True),
           r=["cbs", "onesq"], w=["ps_s1"])
        OP("ve", lambda e: e.tensor_tensor(cbs[:, 7:8], ps_s1[:, 2:3],
                                           cbs[:, 2:3], ALU.is_ge),
           r=["ps_s1"], w=["cbs"])
        OP("ve", lambda e: e.tensor_tensor(wk4c, cbs[:, 4:5], cbs[:, 5:6],
                                           ALU.subtract), r=["cbs"], w=["rt_wk4"])
        OP("ve", lambda e: e.tensor_tensor(wk4c, wk4c, cbs[:, 7:8], ALU.mult),
           r=["cbs"], w=["rt_wk4"])
        OP("ve", lambda e: e.tensor_tensor(cbs[:, 5:6], cbs[:, 5:6], wk4c,
                                           ALU.add), r=["rt_wk4"], w=["cbs"])
        OP("ve", lambda e: e.tensor_scalar(cbs[:, 0:1], cbs[:, 7:8], -1.0, 1.0,
                                           ALU.mult, ALU.add), r=[], w=["cbs"])
        OP("ve", lambda e: e.tensor_tensor(wk4c, cbs[:, 6:7], cbs[:, 4:5],
                                           ALU.subtract), r=["cbs"], w=["rt_wk4"])
        OP("ve", lambda e: e.tensor_tensor(wk4c, wk4c, cbs[:, 0:1], ALU.mult),
           r=["cbs"], w=["rt_wk4"])
        OP("ve", lambda e: e.tensor_tensor(cbs[:, 6:7], cbs[:, 6:7], wk4c,
                                           ALU.subtract), r=["rt_wk4"], w=["cbs"])
    OP("ve", lambda e: e.tensor_scalar(RTT["wk1"], RTT["su"], cbs[:, 5:6],
                                       None, ALU.is_ge),
       r=["rt_su", "cbs"], w=["rt_wk1"])
    OP("ve", lambda e: e.tensor_scalar(RTT["wk2"], RTT["su"], cbs[:, 6:7],
                                       None, ALU.is_lt),
       r=["rt_su", "cbs"], w=["rt_wk2"])
    OP("ve", lambda e: e.tensor_tensor(RTT["wk1"], RTT["wk1"], RTT["wk2"],
                                       ALU.mult), r=["rt_wk2"], w=["rt_wk1"])
    OP("ve", lambda e: e.tensor_tensor(RTT["wk2"], RTT["su"], RTT["wk1"],
                                       ALU.mult), r=["rt_su", "rt_wk1"],
       w=["rt_wk2"])
    preduce(RTT["wk2"], "rt_wk2", 18, op=ALU.max)
    bcast(18, 4)
    OP("ve", lambda e: e.tensor_scalar(RTT["wk1"], RTT["su"], cbs[:, 4:5],
                                       None, ALU.is_gt),
       r=["rt_su", "cbs"], w=["rt_wk1"])
    OP("ve", lambda e: e.reduce_sum(cbs[:, 0:1], RTT["wk1"], axis=AX.X),
       r=["rt_wk1"], w=["cbs"])
    OP("pe", lambda e: e.matmul(ps_s1[:, 2:3], onesq, cbs[:, 0:1],
                                start=True, stop=True),
       r=["cbs", "onesq"], w=["ps_s1"])
    OP("ve", lambda e: e.tensor_copy(RTT["offsf"][:, 0:1], ps_s1[:, 2:3]),
       r=["ps_s1"], w=["rt_offsf"])
    OP("ve", lambda e: e.tensor_scalar(RTT["wk2"], RTT["su"], cbs[:, 4:5],
                                       None, ALU.is_equal),
       r=["rt_su", "cbs"], w=["rt_wk2"])
    prefix2(RTT["wk2"], "rt_wk2", RTT["rank"], "rt_rank", glob=True)
    OP("ve", lambda e: e.tensor_tensor(cbs[:, 3:4], cbs[:, 2:3],
                                       RTT["offsf"][:, 0:1], ALU.subtract),
       r=["rt_offsf"], w=["cbs"])
    OP("ve", lambda e: e.tensor_scalar(RTT["wk3"], RTT["rank"], cbs[:, 3:4],
                                       None, ALU.is_lt),
       r=["rt_rank", "cbs"], w=["rt_wk3"])
    OP("ve", lambda e: e.tensor_tensor(RTT["wk3"], RTT["wk3"], RTT["wk2"],
                                       ALU.mult), r=["rt_wk2"], w=["rt_wk3"])
    OP("ve", lambda e: e.tensor_tensor(RTT["wk3"], RTT["wk3"], RTT["wk1"],
                                       ALU.add), r=["rt_wk1"], w=["rt_wk3"])
    OP("ve", lambda e: e.tensor_scalar(cbs[:, 3:4], cbs[:, 2:3], 1.0,
                                       None, ALU.is_ge), r=[], w=["cbs"])
    OP("ve", lambda e: e.scalar_tensor_tensor(
        RTT["selt"], RTT["wk3"], cbs[:, 3:4], RTT["selt"], ALU.mult, ALU.add),
       r=["rt_wk3", "cbs"], w=["rt_selt"])

    prefix2(RTT["selt"], "rt_selt", RTT["rank"], "rt_rank", glob=False)
    OP("ve", lambda e: e.tensor_scalar(RTT["wk1"], RTT["selt"], -HUGE, HUGE,
                                       ALU.mult, ALU.add),
       r=["rt_selt"], w=["rt_wk1"])
    OP("ve", lambda e: e.tensor_tensor(RTT["offsf"], RTT["rank"], RTT["selt"],
                                       ALU.mult),
       r=["rt_rank", "rt_selt"], w=["rt_offsf"])
    OP("ve", lambda e: e.tensor_tensor(RTT["offsf"], RTT["offsf"], RTT["wk1"],
                                       ALU.add), r=["rt_wk1"], w=["rt_offsf"])
    OP("ve", lambda e: e.tensor_scalar(RTT["wk1"][:, 0:32], RTT["offsf"][:, 0:32],
                                       hsb[:, 0:1], None, ALU.mult),
       r=["rt_offsf", "hsb"], w=["rt_wk1"])
    OP("ve", lambda e: e.scalar_tensor_tensor(
        RTT["wk2"][:, 0:32], RTT["offsf"][:, 32:64], hsb[:, 1:2],
        RTT["wk1"][:, 0:32], ALU.mult, ALU.add),
       r=["rt_offsf", "hsb", "rt_wk1"], w=["rt_wk2"])
    OP("ve", lambda e: e.tensor_copy(offs_i, RTT["wk2"][:, 0:32]),
       r=["rt_wk2"], w=["offs_i"])
    OP("gp", lambda e: e.dma_start(
        out=offs_flat_d.rearrange("(p f) -> p f", f=32), in_=offs_i),
       r=["offs_i"], w=["offs_flat"], dma=True)
    with nc.allow_non_contiguous_dma(reason="one-time 16KB tile-major offs read"):
        OP("gp", lambda e: e.dma_start(
            out=offstm_i, in_=offs_flat_d.rearrange("(f p) -> p f", p=128),
            allow_slow_non_contiguous=True),
           r=["offs_flat"], w=["offstm_i"], dma=True)
    if dbg:
        OP("gp", lambda e: e.dma_start(out=btdbg_d[:, 0:8], in_=BT["wocb2"]),
           r=["btall"], w=["btd"], dma=True)
        OP("gp", lambda e: e.dma_start(out=btdbg_d[:, 8:16], in_=BT["hwe"]),
           r=["btall"], w=["btd"], dma=True)
        for i2, nm2 in enumerate(["s_pm", "u_pm", "selt", "offsf", "scores"]):
            OP("gp", lambda e, i2=i2, nm2=nm2: e.dma_start(
                out=sdbg_d[:, i2 * RT:(i2 + 1) * RT], in_=RTT[nm2]),
               r=["rt_" + nm2], w=["sdbg"], dma=True)
        OP("gp", lambda e: e.dma_start(out=ahist_d, in_=ahist),
           r=["ahist"], w=["ahd"], dma=True)
        OP("ve", lambda e: e.tensor_copy(cbs[:, 0:1], scl[0:1, 0:64].rearrange("o (a b) -> o a b", a=1) if False else cbs[:, 0:1]),
           r=[], w=["cbs"]) if False else None
        OP("gp", lambda e: e.dma_start(out=sclcbs_d[0:1, 0:16], in_=scl[0:1, 0:16]),
           r=["scl"], w=["scd"], dma=True)
        OP("gp", lambda e: e.dma_start(out=sclcbs_d[1:2, 0:16], in_=scl[0:1, 16:32]),
           r=["scl"], w=["scd"], dma=True)

    OP = c.op
    DRIP_K = len(SELQ) // 50 + 1

    def drip(n):
        while SELQ and n > 0:
            ek2, emit2, r2, w2, d2 = SELQ.pop(0)
            c.op(ek2, emit2, r=r2, w=w2, dma=d2)
            n -= 1

    BQ = list(range(4, NTO))

    def bounce_one(gt):
        xv, xb = (xbb0, "xbb0") if gt % 2 == 0 else (xbb1, "xbb1")
        c.op("sp", lambda e, gt=gt, xv=xv: e.dma_start(
            out=xv, in_=xnrtm_d[(gt - 4) * 128:(gt - 3) * 128, :]),
            r=["xnrtm_d"], w=[xb], dma=True)
        c.op("gp", lambda e, gt=gt, xv=xv: e.indirect_dma_start(
            out=comp_d,
            out_offset=IndirectOffsetOnAxis(offstm_i[:, gt - 4:gt - 3], 0),
            in_=xv, in_offset=None,
            bounds_check=bcreg, oob_is_err=False),
            r=[xb, "offstm_i", "comp_d"], w=["comp_d"], dma=True)

    def bdrip(max_gt, n=1):
        while BQ and not SELQ and BQ[0] <= max_gt and n > 0:
            bounce_one(BQ.pop(0))
            n -= 1

    c.fence()

    OP("ve", lambda e: e.tensor_scalar(muall, statS, 1.0 / D, None, ALU.mult),
       r=["statS"], w=["muall"])
    OP("ve", lambda e: e.tensor_scalar(tmp36a, statQ, 1.0 / D, None, ALU.mult),
       r=["statQ"], w=["tmp36a"])
    OP("ve", lambda e: e.tensor_tensor(tmp36b, muall, muall, ALU.mult),
       r=["muall"], w=["tmp36b"])
    OP("ve", lambda e: e.tensor_tensor(tmp36a, tmp36a, tmp36b, ALU.subtract),
       r=["tmp36b"], w=["tmp36a"])
    OP("sc", lambda e: e.activation(rstdall, tmp36a, AF.Ln, bias=epscol),
       r=["tmp36a", "epscol"], w=["rstdall"])
    OP("sc", lambda e: e.activation(rstdall, rstdall, AF.Exp, scale=-0.5),
       r=[], w=["rstdall"])
    OP("ve", lambda e: e.scalar_tensor_tensor(
        tmp36b, tmp36b, -float(D), statQ, ALU.mult, ALU.add),
       r=["statQ"], w=["tmp36b"])
    OP("ve", lambda e: e.tensor_tensor(tmp36a, rstdall, rstdall, ALU.mult),
       r=["rstdall"], w=["tmp36a"])
    OP("ve", lambda e: e.tensor_tensor(tmp36b, tmp36b, tmp36a, ALU.mult),
       r=["tmp36a"], w=["tmp36b"])
    OP("sc", lambda e: e.activation(tmp36b, tmp36b, AF.Ln), r=[], w=["tmp36b"])
    OP("sc", lambda e: e.activation(tmp36b, tmp36b, AF.Exp, scale=0.5),
       r=[], w=["tmp36b"])
    OP("ve", lambda e: e.tensor_scalar(tmp36b, tmp36b, 1e-12, None, ALU.max),
       r=[], w=["tmp36b"])
    OP("ve", lambda e: e.reciprocal(tmp36a, tmp36b), r=["tmp36b"], w=["tmp36a"])
    OP("sc", lambda e: e.copy(tmp36b, tmp36a), r=["tmp36a"], w=["tmp36b"])
    OP("ve", lambda e: e.scalar_tensor_tensor(
        rrall, tmp36b, 32.0, rstdall, ALU.mult, ALU.mult),
       r=["tmp36b", "rstdall"], w=["rrall"])
    OP("ve", lambda e: e.tensor_scalar(carry, BT["wab"], 0.0, None, ALU.mult),
       r=["btall"], w=["carry"])
    if dbg:
        for i3, (tt3, bb3) in enumerate([(statS, "statS"), (statQ, "statQ"),
                                         (tmp36a, "tmp36a"), (tmp36b, "tmp36b"),
                                         (muall, "muall"), (rstdall, "rstdall"),
                                         (rrall, "rrall")]):
            OP("gp", lambda e, i3=i3, tt3=tt3: e.dma_start(
                out=st36_d[:, i3 * NTO:(i3 + 1) * NTO], in_=tt3),
               r=[bb3], w=["st36"], dma=True)

    c.fence()

    for nm, src, dst in [("wa_sb", wa16, wa_sb), ("wb_sb", wb16, wb_sb),
                         ("wo_sb", wo16, wo_sb)]:
        OP("sp", lambda e, s=src, d=dst: e.dma_start(
            out=d.rearrange("p (kc m) -> p kc m", m=1024),
            in_=s.rearrange("(kc p) m -> p kc m", p=128)),
           r=[nm[:2] + "16"], w=[nm], dma=True)
    OP("sp", lambda e: e.dma_start(out=wmrow, in_=wmask_i), w=["wmrow"], dma=True)
    OP("pe", lambda e: e.matmul(ps_x, orow, wmrow, start=True, stop=True),
       r=["orow", "wmrow"], w=["ps_x"])
    OP("ve", lambda e: e.tensor_copy(wmbc, ps_x), r=["ps_x"], w=["wmbc"])

    for ck in range(NCK + 1):
        real = ck >= 1
        tokr = (ck - 1) * TC
        for i in range(4):
            gt = ck * 4 + i
            xt = xbuf[:, (i % 2) * 1024:(i % 2) * 1024 + 1024]
            xb = "xbuf%d" % (i % 2)
            OP("sp", lambda e, gt=gt, xt=xt: e.dma_start(
                out=xt, in_=xo[gt * 128:(gt + 1) * 128, :]), w=[xb], dma=True)
            drip(DRIP_K)
            bdrip(gt - 1)
            OP("ve", lambda e, xt=xt, gt=gt: e.tensor_scalar(
                xntm, xt, muall[:, gt:gt + 1], rstdall[:, gt:gt + 1],
                ALU.subtract, ALU.mult),
               r=[xb, "muall", "rstdall"], w=["xntm"])
            if dbg:
                OP("gp", lambda e, gt=gt: e.dma_start(
                    out=xndbg_d[gt * 128:(gt + 1) * 128, :], in_=xntm),
                   r=["xntm"], w=["xnd"], dma=True)
            if real:
                OP("ve", lambda e, xt=xt, gt=gt: e.tensor_scalar(
                    xnrtm, xt, muall[:, gt:gt + 1], rrall[:, gt:gt + 1],
                    ALU.subtract, ALU.mult),
                   r=[xb, "muall", "rrall"], w=["xnrtm"])
                OP("gp", lambda e, gt=gt: e.dma_start(
                    out=xnrtm_d[(gt - 4) * 128:(gt - 3) * 128, :], in_=xnrtm),
                   r=["xnrtm"], w=["xnrtm_d"], dma=True)

            def trans8(e, src_ap=None):
                last = None
                for j in range(8):
                    last = e.transpose(ps_tr[:, j * 128:(j + 1) * 128],
                                       src_ap[:, j * 128:(j + 1) * 128], idb_t)
                return last

            def evac8(e, dst_ap=None, i=i):
                last = None
                for j in range(8):
                    last = e.copy(dst_ap[:, j * 512 + i * 128:j * 512 + i * 128 + 128],
                                  ps_tr[:, j * 128:(j + 1) * 128])
                return last

            OP("pe", lambda e: trans8(e, src_ap=xntm), r=["xntm", "idb_t"],
               w=["ps_tr"])
            OP("sc", lambda e: evac8(e, dst_ap=xnfm), r=["ps_tr"], w=["xnfm"])
            if real:
                OP("pe", lambda e: trans8(e, src_ap=xnrtm), r=["xnrtm", "idb_t"],
                   w=["ps_tr"])
                OP("sc", lambda e: evac8(e, dst_ap=xnrfm), r=["ps_tr"],
                   w=["xnrfm"])
        if real:
            OP("gp", lambda e, tokr=tokr: e.dma_start(
                out=xnrfm_d[:, :, tokr:tokr + TC].rearrange("cc p t -> p cc t"),
                in_=xnrfm.rearrange("p (cc t) -> p cc t", t=TC)),
               r=["xnrfm"], w=["xnrfm_d"], dma=True)

        def mm8(e, ps=None, slab=None, cc2=0, rhs=None):
            last = None
            for kc in range(NCH):
                last = e.matmul(ps,
                                slab[:, kc * 1024 + cc2 * 128:kc * 1024 + cc2 * 128 + 128],
                                rhs[:, kc * 512:(kc + 1) * 512],
                                start=(kc == 0), stop=(kc == NCH - 1))
            return last

        for cc in range(NCH):
            drip(DRIP_K)
            bdrip(ck * 4 + 3)
            OP("pe", lambda e, cc=cc: mm8(e, ps=ps_a, slab=wa_sb, cc2=cc,
                                          rhs=xnfm),
               r=["wa_sb", "xnfm"], w=["ps_a"])
            OP("sc", lambda e, cc=cc: e.activation(
                acc_s, ps_a, AF.Tanh, bias=BT["wab"][:, cc:cc + 1]),
               r=["ps_a", "btall"], w=["acc_s"])
            OP("pe", lambda e, cc=cc: mm8(e, ps=ps_b, slab=wb_sb, cc2=cc,
                                          rhs=xnfm),
               r=["wb_sb", "xnfm"], w=["ps_b"])
            OP("ve", lambda e, cc=cc: e.tensor_scalar(
                bcc_s, ps_b, BT["wbb"][:, cc:cc + 1], None, ALU.add),
               r=["ps_b", "btall"], w=["bcc_s"])
            if ck == 0:
                OP("ve", lambda e: e.tensor_tensor(acc_s, acc_s, wmbc, ALU.mult),
                   r=["wmbc"], w=["acc_s"])
                OP("ve", lambda e: e.tensor_tensor(bcc_s, bcc_s, wmbc, ALU.mult),
                   r=["wmbc"], w=["bcc_s"])

            OP("ve", lambda e, cc=cc: e.tensor_tensor_scan(
                hcc_s, acc_s, bcc_s, carry[:, cc:cc + 1], ALU.mult, ALU.add),
               r=["acc_s", "bcc_s", "carry"], w=["hcc_s"])
            OP("ve", lambda e, cc=cc: e.tensor_copy(carry[:, cc:cc + 1],
                                                    hcc_s[:, 511:512]),
               r=["hcc_s"], w=["carry"])
            if real:
                OP("sc", lambda e, cc=cc, tokr=tokr: e.copy(
                    h_res[:, cc * TOK + tokr:cc * TOK + tokr + TC], hcc_s),
                   r=["hcc_s"], w=["h_res"])
                wp_half = (cc % 2) * 512
                OP("pe", lambda e, cc=cc: mm8(e, ps=ps_x, slab=wo_sb, cc2=cc,
                                              rhs=xnfm),
                   r=["wo_sb", "xnfm"], w=["ps_x"])
                if dbg and ck == 1 and cc == 0:
                    OP("ve", lambda e: e.tensor_copy(xr_s[:, 0:512], ps_x),
                       r=["ps_x"], w=["xr_s"])
                    OP("gp", lambda e: e.dma_start(out=wraw_d, in_=xr_s[:, 0:512]),
                       r=["xr_s"], w=["wrawd"], dma=True)
                OP("ve", lambda e, cc=cc, wp_half=wp_half: e.tensor_scalar(
                    wpout[:, wp_half:wp_half + 512], ps_x,
                    BT["wocb2"][:, cc:cc + 1], BT["hwe"][:, cc:cc + 1],
                    ALU.add, ALU.mult),
                   r=["ps_x", "btall"], w=["wpout%d" % (cc % 2)])
                OP("gp", lambda e, cc=cc, tokr=tokr, wp_half=wp_half: e.dma_start(
                    out=wpfm_d[cc, :, tokr:tokr + TC],
                    in_=wpout[:, wp_half:wp_half + 512]),
                   r=["wpout%d" % (cc % 2)], w=["wpfm_d"], dma=True)
    drip(10 ** 9)
    bdrip(10 ** 9, n=10 ** 9)
    if dbg:
        OP("gp", lambda e: e.dma_start(
            out=hdbg_d.rearrange("cc p t -> p cc t"),
            in_=h_res.rearrange("p (cc t) -> p cc t", t=TOK)),
           r=["h_res"], w=["hdbg"], dma=True)

    c.fence()

    for g in range(8):
        ctm = comptm[:, (g % 2) * 1024:(g % 2) * 1024 + 1024]
        cb2 = "comptm%d" % (g % 2)
        OP("sp", lambda e, g=g, ctm=ctm: e.dma_start(
            out=ctm, in_=comp_d[g * 128:(g + 1) * 128, :]),
           r=["comp_d"], w=[cb2], dma=True)

        def tr8(e, ctm=ctm):
            last = None
            for j in range(8):
                last = e.transpose(ps_tr[:, j * 128:(j + 1) * 128],
                                   ctm[:, j * 128:(j + 1) * 128], idb_t)
            return last

        def ev8(e, g=g):
            last = None
            for j in range(8):
                last = e.copy(compfm[:, j * 1024 + g * 128:j * 1024 + g * 128 + 128],
                              ps_tr[:, j * 128:(j + 1) * 128])
            return last

        OP("pe", lambda e, ctm=ctm: tr8(e, ctm=ctm), r=[cb2, "idb_t"], w=["ps_tr"])
        OP("sc", ev8, r=["ps_tr"], w=["compfm"])

    c.fence()
    for hh2 in range(2):
        for kc in range(NCH):
            OP("sp", lambda e, kc=kc, hh2=hh2: e.dma_start(
                out=hw1sl[:, kc * 2048:(kc + 1) * 2048],
                in_=hw116[kc * 128:(kc + 1) * 128, hh2 * 2048:(hh2 + 1) * 2048]),
               r=["hw116"], w=["hw1sl%d" % kc], dma=True)
        PSH = (ps_a, ps_b, ps_cd[:, 0:512], ps_cd[:, 512:1024])
        PBH = ("ps_a", "ps_b", "ps_cd", "ps_cd")
        for og in range(4):
            for half in range(2):
                for q in range(4):
                    oq = og * 4 + q
                    occ = hh2 * 16 + oq
                    ps = PSH[q]
                    pbuf = PBH[q]

                    def mm1(e, oq=oq, half=half, ps=ps):
                        last = None
                        for kc in range(NCH):
                            last = e.matmul(
                                ps, hw1sl[:, kc * 2048 + oq * 128:kc * 2048 + oq * 128 + 128],
                                compfm[:, kc * 1024 + half * 512:kc * 1024 + half * 512 + 512],
                                start=(kc == 0), stop=(kc == NCH - 1))
                        return last

                    OP("pe", mm1,
                       r=["hw1sl%d" % kc3 for kc3 in range(NCH)] + ["compfm"],
                       w=[pbuf])
                    OP("sc", lambda e, occ=occ, ps=ps, half=half: e.activation(
                        big_u[:, occ * 1024 + half * 512:occ * 1024 + half * 512 + 512],
                        ps, AF.Gelu, bias=BT["hb1"][:, occ:occ + 1]),
                       r=[pbuf, "btall"], w=["big_u"])
    c.fence()
    OP("sp", lambda e: e.dma_start(
        out=hvot[0:1, 0:1024], in_=b_in["hb2"].rearrange("(o d) -> o d", o=1)),
       w=["hvot0"], dma=True)
    OP("sc", lambda e: e.copy(hb2b, hvot[0:1, 0:1024]), r=["hvot0"], w=["hb2b"])
    for g in range(8):
        for kc2 in range(NHC):
            sl = hw2sl[:, (kc2 % 4) * 1024:(kc2 % 4) * 1024 + 1024]
            sb2 = "hw2sl%d" % (kc2 % 4)
            OP("sp", lambda e, kc2=kc2, sl=sl: e.dma_start(
                out=sl, in_=hw216[kc2 * 128:(kc2 + 1) * 128, :]),
               r=["hw216"], w=[sb2], dma=True)

            def mmw2(e, kc2=kc2, g=g, sl=sl):
                last = None
                for half in range(2):
                    last = e.matmul(
                        ps_cd[:, half * 512:half * 512 + 512],
                        big_u[:, kc2 * 1024 + g * 128:kc2 * 1024 + g * 128 + 128],
                        sl[:, half * 512:half * 512 + 512],
                        start=(kc2 == 0), stop=False)
                return last

            OP("pe", mmw2, r=[sb2, "big_u"], w=["ps_cd"])
        def mmb2(e):
            e.matmul(ps_cd[:, 0:512], orowb, hb2b[0:1, 0:512],
                     start=False, stop=True, skip_group_check=True)
            return e.matmul(ps_cd[:, 512:1024], orowb, hb2b[0:1, 512:1024],
                            start=False, stop=True, skip_group_check=True)

        OP("pe", mmb2, r=["orowb", "hb2b"], w=["ps_cd"])
        hv = hvot[:, (g % 2) * 1024:(g % 2) * 1024 + 1024]
        hb3 = "hvot%d" % (g % 2)
        OP("sc", lambda e, hv=hv: e.copy(hv, ps_cd), r=["ps_cd"], w=[hb3])
        OP("gp", lambda e, g=g, hv=hv: e.dma_start(
            out=hvo_d[g * 128:(g + 1) * 128, :], in_=hv),
           r=[hb3], w=["hvo_d"], dma=True)

    c.fence()

    OP("sp", lambda e: e.dma_start(
        out=cs_sb.rearrange("p (kc m) -> p kc m", m=1024),
        in_=c16.rearrange("(kc p) m -> p kc m", p=128)),
       r=["c16"], w=["cs_sb"], dma=True)
    OP("ve", lambda e: e.tensor_scalar(prow[0:1, 1024:1536], lngrow[0:1, 0:512],
                                       0.0, 1.0, ALU.mult, ALU.add),
       r=["lngrow"], w=["prow"])
    OP("sc", lambda e: e.copy(prowb[0:1, 1024:1536], prow[0:1, 1024:1536]),
       r=["prow"], w=["prowb"])
    for ck in range(NCK):
        tokr = ck * TC
        wps = wp_sb[:, (ck % 2) * 4096:(ck % 2) * 4096 + 4096]
        wpb = "wp_sb%d" % (ck % 2)
        OP("sp", lambda e, tokr=tokr, wps=wps: e.dma_start(
            out=wps.rearrange("p (cc t) -> p cc t", t=512),
            in_=wpfm_d[:, :, tokr:tokr + TC].rearrange("cc p t -> p cc t")),
           r=["wpfm_d"], w=[wpb], dma=True)

        def mmc(e, cc=0, tokr=tokr, ps=None):
            last = None
            for kc in range(NCH):
                last = e.matmul(
                    ps, cs_sb[:, kc * 1024 + cc * 128:kc * 1024 + cc * 128 + 128],
                    h_res[:, kc * TOK + tokr:kc * TOK + tokr + TC],
                    start=(kc == 0), stop=(kc == NCH - 1))
            return last

        def mmst(e, cc=0):
            zp = (cc % 2) * 512
            e.matmul(ps_s1[0:1, :], ones1b, zb16[:, zp:zp + 512],
                     start=(cc == 0), stop=(cc == NCH - 1),
                     skip_group_check=True)
            return e.matmul(ps_s2, ones1b, zsqb[:, zp:zp + 512],
                            start=(cc == 0), stop=(cc == NCH - 1),
                            skip_group_check=True)

        for cc in range(NCH + 1):
            if cc < NCH:
                ps = (ps_a, ps_b)[cc % 2]
                pbuf = ("ps_a", "ps_b")[cc % 2]
                OP("pe", lambda e, cc=cc, ps=ps: mmc(e, cc=cc, ps=ps),
                   r=["cs_sb", "h_res"], w=[pbuf])
                OP("ve", lambda e, cc=cc, ps=ps, wps=wps: e.scalar_tensor_tensor(
                    z_sb[:, cc * 512:(cc + 1) * 512], ps, BT["hwe"][:, cc:cc + 1],
                    wps[:, cc * 512:(cc + 1) * 512], ALU.mult, ALU.add),
                   r=[pbuf, "btall", wpb], w=["z_sb"])
                zp = (cc % 2) * 512
                OP("sc", lambda e, cc=cc, zp=zp: e.copy(
                    zb16[:, zp:zp + 512], z_sb[:, cc * 512:(cc + 1) * 512]),
                   r=["z_sb"], w=["zb16"])
                OP("sc", lambda e, cc=cc, zp=zp: e.activation(
                    zsqb[:, zp:zp + 512], z_sb[:, cc * 512:(cc + 1) * 512],
                    AF.Square),
                   r=["z_sb"], w=["zsqb"])
            if cc >= 1:
                OP("pe", lambda e, cc=cc: mmst(e, cc=cc - 1),
                   r=["ones1b", "zb16", "zsqb"], w=["ps_s1", "ps_s2"])
        OP("ve", lambda e: e.tensor_scalar(prow[0:1, 1536:2048], ps_s1[0:1, :], 1.0 / D,
                                           None, ALU.mult),
           r=["ps_s1"], w=["prow"])
        OP("ve", lambda e: e.tensor_scalar(prow[0:1, 512:1024], ps_s2, 1.0 / D,
                                           None, ALU.mult),
           r=["ps_s2"], w=["prow"])
        OP("ve", lambda e: e.tensor_tensor(prow[0:1, 0:512], prow[0:1, 1536:2048],
                                           prow[0:1, 1536:2048], ALU.mult),
           r=[], w=["prow"])
        OP("ve", lambda e: e.tensor_tensor(prow[0:1, 512:1024], prow[0:1, 512:1024],
                                           prow[0:1, 0:512], ALU.subtract),
           r=[], w=["prow"])
        OP("sc", lambda e: e.activation(prow[0:1, 0:512], prow[0:1, 512:1024],
                                        AF.Ln, bias=epscol[0:1, 0:1]),
           r=["epscol"], w=["prow"])
        OP("sc", lambda e: e.activation(prow[0:1, 0:512], prow[0:1, 0:512],
                                        AF.Exp, scale=-0.5), r=[], w=["prow"])
        OP("ve", lambda e: e.tensor_tensor(prow[0:1, 512:1024], prow[0:1, 1536:2048],
                                           prow[0:1, 0:512], ALU.mult),
           r=[], w=["prow"])
        OP("ve", lambda e: e.tensor_scalar(prow[0:1, 512:1024], prow[0:1, 512:1024],
                                           -1.0, None, ALU.mult), r=[], w=["prow"])
        OP("sc", lambda e: e.copy(prowb[0:1, 0:512], prow[0:1, 0:512]),
           r=["prow"], w=["prowb"])
        OP("sc", lambda e: e.copy(prowb[0:1, 512:1024], prow[0:1, 512:1024]),
           r=["prow"], w=["prowb"])
        for cc in range(NCH):
            def mmpq(e, cc=cc):
                e.matmul(ps_cd[:, 0:512], lngrowb[0:1, cc * 128:(cc + 1) * 128],
                         prowb[0:1, 0:512], start=True, stop=False)
                e.matmul(ps_cd[:, 0:512], orowb, prowb[0:1, 1024:1536],
                         start=False, stop=True, skip_group_check=True)
                e.matmul(ps_cd[:, 512:1024], lngrowb[0:1, cc * 128:(cc + 1) * 128],
                         prowb[0:1, 512:1024], start=True, stop=False)
                return e.matmul(ps_cd[:, 512:1024],
                                lnbrowb[0:1, cc * 128:(cc + 1) * 128],
                                prowb[0:1, 1024:1536], start=False, stop=True,
                                skip_group_check=True)

            OP("pe", mmpq, r=["lngrowb", "lnbrowb", "orowb", "prowb"],
               w=["ps_cd"])
            OP("ve", lambda e, cc=cc: e.tensor_tensor(
                zsqb.bitcast(F32)[:, 0:512] if False else z_sb[:, cc * 512:(cc + 1) * 512],
                z_sb[:, cc * 512:(cc + 1) * 512], ps_cd[:, 0:512], ALU.mult),
               r=["ps_cd"], w=["z_sb"])
            OP("ve", lambda e, cc=cc, tokr=tokr: e.tensor_tensor(
                big_u[:, cc * TOK + tokr:cc * TOK + tokr + TC],
                z_sb[:, cc * 512:(cc + 1) * 512], ps_cd[:, 512:1024],
                ALU.add), r=["z_sb", "ps_cd"], w=["big_u"])
    if dbg:
        OP("gp", lambda e: e.dma_start(
            out=udbg_d.rearrange("cc p t -> p cc t"),
            in_=big_u.rearrange("p (cc t) -> p cc t", t=TOK)),
           r=["big_u"], w=["udbg"], dma=True)

    c.fence()

    OP("sp", lambda e: e.dma_start(
        out=xb2w[0:1, 0:1024], in_=b_in["pb"].rearrange("(o d) -> o d", o=1)),
       w=["xb2ws"], dma=True)
    OP("sp", lambda e: e.dma_start(
        out=ytile[0:1, 0:1024], in_=b_in["lb2"].rearrange("(o d) -> o d", o=1)),
       w=["ytile"], dma=True)
    OP("ve", lambda e: e.tensor_tensor(ytile[0:1, 0:1024], xb2w[0:1, 0:1024],
                                       ytile[0:1, 0:1024], ALU.add),
       r=["xb2ws"], w=["ytile"])
    OP("sc", lambda e: e.copy(pbl2b, ytile[0:1, 0:1024]),
       r=["ytile"], w=["pbl2b"])
    OP("sp", lambda e: e.dma_start(
        out=lw1_sb.rearrange("p (kc m) -> p kc m", m=512),
        in_=lw116.rearrange("(kc p) m -> p kc m", p=128)),
       r=["lw116"], w=["lw1_sb"], dma=True)
    def f1load(kc, hh2, q="sp"):
        wl1 = (["f1sl0", "f2sl0", "f2sl1"] if kc == 0 else
               ["f1sl1", "f2sl2", "f2sl3"] if kc == 1 else
               ["f1sl%d" % kc])
        OP(q, lambda e, kc=kc, hh2=hh2: e.dma_start(
            out=f1sl[:, kc * 2048:(kc + 1) * 2048],
            in_=f116[kc * 128:(kc + 1) * 128, hh2 * 2048:(hh2 + 1) * 2048]),
           r=["f116"], w=wl1, dma=True)

    for kc in range(NCH):
        f1load(kc, 0)
    for ck in range(NCK):
        tokr = ck * TC
        OP("sp", lambda e, tokr=tokr: e.dma_start(
            out=xnrf_sb.rearrange("p (cc t) -> p cc t", t=512),
            in_=xnrfm_d[:, :, tokr:tokr + TC].rearrange("cc p t -> p cc t")),
           r=["xnrfm_d"], w=["xnrf_sb"], dma=True)
        def light1(lc):
            ps = (ps_a, ps_b)[lc % 2]
            pbuf = ("ps_a", "ps_b")[lc % 2]

            def mml1(e, lc=lc, ps=ps):
                last = None
                for kc in range(NCH):
                    last = e.matmul(
                        ps, lw1_sb[:, kc * 512 + lc * 128:kc * 512 + lc * 128 + 128],
                        xnrf_sb[:, kc * 512:(kc + 1) * 512],
                        start=(kc == 0), stop=(kc == NCH - 1))
                return last

            OP("pe", mml1, r=["lw1_sb", "xnrf_sb"], w=[pbuf])
            OP("sc", lambda e, lc=lc, ps=ps: e.activation(
                lh_sb[:, lc * 512:(lc + 1) * 512], ps, AF.Gelu,
                bias=BT["lb1"][:, lc:lc + 1]),
               r=[pbuf, "btall"], w=["lh_sb"])

        PS4 = (ps_a, ps_b, ps_cd[:, 0:512], ps_cd[:, 512:1024])
        PB4 = ("ps_a", "ps_b", "ps_cd", "ps_cd")
        for hh2 in range(2):
            if hh2 == 1:
                for kc in range(NCH):
                    f1load(kc, 1)
            for hq in range(16):
                hc = hh2 * 16 + hq
                ps = PS4[hq % 4]
                pbuf = PB4[hq % 4]

                def mmf1(e, hq=hq, tokr=tokr, ps=ps):
                    last = None
                    for kc in range(NCH):
                        last = e.matmul(
                            ps, f1sl[:, kc * 2048 + hq * 128:kc * 2048 + hq * 128 + 128],
                            big_u[:, kc * TOK + tokr:kc * TOK + tokr + TC],
                            start=(kc == 0), stop=(kc == NCH - 1))
                    return last

                OP("pe", mmf1,
                   r=["f1sl%d" % kc3 for kc3 in range(NCH)] + ["big_u"],
                   w=[pbuf])
                OP("sc", lambda e, hc=hc, ps=ps: e.activation(
                    f1h[:, hc * 512:(hc + 1) * 512], ps, AF.Gelu,
                    bias=BT["f1b"][:, hc:hc + 1]),
                   r=[pbuf, "btall"], w=["f1h"])
            light1(hh2 * 2)
            light1(hh2 * 2 + 1)
        if ck + 1 < NCK:
            for kc in range(2, NCH):
                f1load(kc, 0, q="s2")
        for fp in range(2):
            for kc2 in range(NHC):
                sl = f2sl[:, (kc2 % 4) * 1024:(kc2 % 4) * 1024 + 1024]
                sb2 = "f2sl%d" % (kc2 % 4)
                wl = ([sb2, "f1sl0"] if (fp == 0 and kc2 < 2) else
                      [sb2, "f1sl1"] if (fp == 0 and kc2 < 4) else [sb2])
                OP("sp", lambda e, kc2=kc2, sl=sl: e.dma_start(
                    out=sl, in_=f216[kc2 * 128:(kc2 + 1) * 128, :]),
                   r=["f216"], w=wl, dma=True)

                def mmf2(e, kc2=kc2, fp=fp, sl=sl):
                    last = None
                    for q in range(4):
                        occ = fp * 4 + q
                        ps = (ps_a, ps_b, ps_cd[:, 0:512], ps_cd[:, 512:1024])[q]
                        last = e.matmul(
                            ps, sl[:, occ * 128:(occ + 1) * 128],
                            f1h[:, kc2 * 512:(kc2 + 1) * 512],
                            start=(kc2 == 0), stop=False)
                    return last

                OP("pe", mmf2, r=[sb2, "f1h"], w=["ps_a", "ps_b", "ps_cd"])
            def mmfu(e, fp=fp, tokr=tokr):
                last = None
                for q in range(4):
                    occ = fp * 4 + q
                    ps = (ps_a, ps_b, ps_cd[:, 0:512], ps_cd[:, 512:1024])[q]
                    last = e.matmul(
                        ps, idb_t, big_u[:, occ * TOK + tokr:occ * TOK + tokr + TC],
                        start=False, stop=True, skip_group_check=True)
                return last

            OP("pe", mmfu, r=["idb_t", "big_u"], w=["ps_a", "ps_b", "ps_cd"])
            for q in range(4):
                occ = fp * 4 + q
                ps = (ps_a, ps_b, ps_cd[:, 0:512], ps_cd[:, 512:1024])[q]
                pbuf = ("ps_a", "ps_b", "ps_cd", "ps_cd")[q]
                OP("sc", lambda e, occ=occ, ps=ps: e.activation(
                    uffn[:, occ * 512:(occ + 1) * 512], ps, AF.Identity,
                    bias=BT["f2b"][:, occ:occ + 1]),
                   r=[pbuf, "btall"], w=["uffn"])
        if ck + 1 < NCK:
            f1load(0, 0, q="s2")
            f1load(1, 0, q="s2")
        for kc2 in range(4):
            OP("sp", lambda e, kc2=kc2: e.dma_start(
                out=lw2sl[:, kc2 * 1024:(kc2 + 1) * 1024],
                in_=lw216[kc2 * 128:(kc2 + 1) * 128, :]),
               r=["lw216"], w=["lw2sl"], dma=True)
        OP("sp", lambda e: e.dma_start(
            out=pt_sb.rearrange("p (kc m) -> p kc m", m=1024),
            in_=p16.rearrange("(kc p) m -> p kc m", p=128)),
           r=["p16"], w=["pt_sb"], dma=True)
        for tg in range(4):
            def mmout(e, tg=tg):
                last = None
                for kc in range(NCH):
                    for half in range(2):
                        last = e.matmul(
                            ps_cd[:, half * 512:half * 512 + 512],
                            uffn[:, kc * 512 + tg * 128:kc * 512 + tg * 128 + 128],
                            pt_sb[:, kc * 1024 + half * 512:kc * 1024 + half * 512 + 512],
                            start=(kc == 0), stop=False)
                return last

            OP("pe", mmout, r=["uffn", "pt_sb"], w=["ps_cd"])

            def mmlw2(e, tg=tg):
                last = None
                for kc2 in range(4):
                    for half in range(2):
                        last = e.matmul(
                            ps_cd[:, half * 512:half * 512 + 512],
                            lh_sb[:, kc2 * 512 + tg * 128:kc2 * 512 + tg * 128 + 128],
                            lw2sl[:, kc2 * 1024 + half * 512:kc2 * 1024 + half * 512 + 512],
                            start=False, stop=False)
                return last

            OP("pe", mmlw2, r=["lh_sb", "lw2sl"], w=["ps_cd"])
            gt = (W + tokr) // 128 + tg
            xbt = xb2w[:, 0:1024]
            xbb = "xb2ws"
            OP("sp", lambda e, gt=gt, xbt=xbt: e.dma_start(
                out=xbt, in_=xo[gt * 128:(gt + 1) * 128, :]),
               w=[xbb], dma=True)

            def mmxb(e):
                e.matmul(ps_cd[:, 0:512], orowb, pbl2b[0:1, 0:512],
                         start=False, stop=True, skip_group_check=True)
                return e.matmul(ps_cd[:, 512:1024], orowb, pbl2b[0:1, 512:1024],
                                start=False, stop=True, skip_group_check=True)

            OP("pe", mmxb, r=["orowb", "pbl2b"], w=["ps_cd"])
            OP("ve", lambda e, xbt=xbt: e.tensor_tensor(ytile, ps_cd, xbt,
                                                        ALU.add),
               r=["ps_cd", xbb], w=["ytile"])
            OP("gp", lambda e, tokr=tokr, tg=tg: e.dma_start(
                out=y[tokr + tg * 128:tokr + tg * 128 + 128, :], in_=ytile),
               r=["ytile"], w=["y_d"], dma=True)

    yv = y.rearrange("(p f) d -> f p d", f=32)
    for f in range(32):
        yt = (ytile, xb2w[:, 0:1024])[f % 2]
        yb = ("ytile", "xb2ws")[f % 2]
        OP("sp", lambda e, f=f, yt=yt: e.dma_start(out=yt, in_=yv[f]),
           r=["y_d"], w=[yb], dma=True)
        OP("gp", lambda e, f=f, yt=yt: e.indirect_dma_start(
            out=yt, out_offset=None,
            in_=hvo_d, in_offset=IndirectOffsetOnAxis(offs_i[:, f:f + 1], 0),
            bounds_check=bcreg, oob_is_err=False, compute_op=ALU.add),
           r=["hvo_d", "offs_i"], w=[yb], dma=True)
        OP("gp", lambda e, f=f, yt=yt: e.dma_start(out=yv[f], in_=yt),
           r=[yb], w=["y_d"], dma=True)

    c.final_wait("gp", ["gp", "sp", "pe", "sc", "ve"])
    return nc


_CACHE = {}


def _get_nc():
    if "nc" not in _CACHE:
        _CACHE["nc"] = build()
    return _CACHE["nc"]


def kernel(**inputs):
    f32 = np.float32
    x = np.ascontiguousarray(inputs["L0_attn_emb"], dtype=f32)
    g = lambda n: np.ascontiguousarray(inputs[n], dtype=f32)

    WaT = np.ascontiguousarray(g("L0_attn_Wa_w").T)
    WbT = np.ascontiguousarray(g("L0_attn_Wb_w").T)
    WoT = np.ascontiguousarray(g("L0_attn_Wo_w").T)
    CT = np.ascontiguousarray(g("L0_attn_C_w").T)
    PT = np.ascontiguousarray(g("L0_attn_p_w").T)
    F1T = np.ascontiguousarray(g("L0_attn_f1_w").T)
    F2T = np.ascontiguousarray(g("L0_attn_f2_w").T)
    ident = np.eye(128, dtype=f32)
    identb = np.eye(128).astype(ml_dtypes.bfloat16)

    u_ix = np.arange(RT)
    p_ix = np.arange(128)
    tmap = np.where(u_ix[None, :] < 32,
                    p_ix[:, None] * 32 + u_ix[None, :],
                    4096 + p_ix[:, None] * 32 + (u_ix[None, :] - 32))
    perm = tmap.T.reshape(-1)

    shared = {
        "waT": WaT, "wbT": WbT, "woT": WoT, "cT": CT, "pTw": PT,
        "f1T": F1T, "f2T": F2T,
        "lw1": g("L0_ffn_light_w1"), "lw2": g("L0_ffn_light_w2"),
        "hw1": g("L0_ffn_heavy_w1"), "hw2": g("L0_ffn_heavy_w2"),
        "idf": ident, "idb": identb,
        "wab": g("L0_attn_Wa_b"), "wbb": g("L0_attn_Wb_b"),
        "wob": g("L0_attn_Wo_b"), "cb": g("L0_attn_C_b"),
        "lng": g("L0_attn_ln_g"), "lnb": g("L0_attn_ln_b"),
        "f2b": g("L0_attn_f2_b"), "pb": g("L0_attn_p_b"),
        "lb2": g("L0_ffn_light_b2"), "hb2": g("L0_ffn_heavy_b2"),
        "hwe": np.ascontiguousarray(np.repeat(g("L0_attn_head_weights"), 64)),
        "rtok": g("L0_ffn_routing_token"),
        "lgam": g("L0_ffn_light_gamma"), "hgam": g("L0_ffn_heavy_gamma"),
        "f1b": g("L0_attn_f1_b"), "hb1": g("L0_ffn_heavy_b1"),
        "lb1": g("L0_ffn_light_b1"),
    }
    def bt(v, n):
        return np.asarray(v, f32).reshape(n, 128).T
    btall = np.zeros((128, 160), f32)
    btall[:, 0:8] = bt(shared["wab"], 8)
    btall[:, 8:16] = bt(shared["wbb"], 8)
    btall[:, 16:24] = bt(shared["wob"], 8)
    btall[:, 24:32] = bt(shared["cb"], 8)
    btall[:, 32:40] = bt(shared["f2b"], 8)
    btall[:, 40:48] = bt(shared["pb"], 8)
    btall[:, 48:56] = bt(shared["hwe"], 8)
    btall[:, 56:64] = bt(shared["lgam"], 8)
    btall[:, 64:72] = bt(shared["hgam"], 8)
    btall[:, 72:104] = bt(shared["f1b"], 32)
    btall[:, 104:136] = bt(shared["hb1"], 32)
    btall[:, 136:140] = bt(shared["lb1"], 4)
    shared["btall"] = np.ascontiguousarray(btall)

    in_maps = []
    for core in range(8):
        b, half = core // 2, core % 2
        t0 = half * 4096
        xr = x[b]
        if half == 0:
            warm = np.zeros((W, D), f32)
            wm = np.zeros((1, W), f32)
        else:
            warm = xr[t0 - W:t0]
            wm = np.ones((1, W), f32)
        m = dict(shared)
        m["xo"] = np.ascontiguousarray(np.concatenate([warm, xr[t0:t0 + 4096]], 0))
        m["xp"] = np.ascontiguousarray(xr[perm])
        m["wmask"] = wm
        m["hs"] = np.array([[1.0 - half, float(half)]], f32)
        hsr = np.zeros((1, 128), f32)
        hsr[0, 0] = 1.0 - half
        hsr[0, 1] = float(half)
        m["hsrow"] = hsr
        in_maps.append(m)

    nc = _get_nc()
    res = run_bass_kernel_spmd(nc, in_maps, core_ids=list(range(8)))
    out = np.empty((4, 8192, 1024), f32)
    for core in range(8):
        b, half = core // 2, core % 2
        out[b, half * 4096:(half + 1) * 4096] = np.asarray(res.results[core]["y"])
    return out
